# version 18
# baseline (speedup 1.0000x reference)
import numpy as np
import ml_dtypes
import concourse.bass as bass
import concourse.mybir as mybir
from concourse.bass_utils import run_bass_kernel_spmd
from contextlib import ExitStack

F32 = mybir.dt.float32
BF16 = mybir.dt.bfloat16
I32 = mybir.dt.int32
AF = mybir.ActivationFunctionType
ALU = mybir.AluOpType

NCORE = 8
TOK = 4096
D = 2048
NST = 1024
HID = 5632
T = 512
G = 4
DC = D // 128
NCH = NST // 128
HC = HID // 128
SL = 4
NXB = 6
EPS = 1e-5

NU_B, NU_C, NU_G, NU_U, NU_D = NCH, SL, HC, HC, HC
NUNITS = NU_B + NU_C + NU_G + NU_U + NU_D


def unit_kind(u):
    if u < NU_B:
        return ("B", u)
    u -= NU_B
    if u < NU_C:
        return ("C", u)
    u -= NU_C
    if u < NU_G:
        return ("G", u)
    u -= NU_G
    if u < NU_U:
        return ("U", u)
    return ("D", u - NU_U)


def build_nc(n_pass):
    nc = bass.Bass("TRN2", target_bir_lowering=False)

    xin = nc.dram_tensor("xin", [TOK, D], F32, kind="ExternalInput").ap()
    wgi = nc.dram_tensor("wgi", [HC, 128, DC * 128], I32, kind="ExternalInput").ap()
    wui = nc.dram_tensor("wui", [HC, 128, DC * 128], I32, kind="ExternalInput").ap()
    wdi = nc.dram_tensor("wdi", [SL * 11, 128, D], I32, kind="ExternalInput").ap()
    bmi = nc.dram_tensor("bmi", [NCH, 128, DC * 128], F32, kind="ExternalInput").ap()
    cmi = nc.dram_tensor("cmi", [SL, 128, NCH * 512], F32, kind="ExternalInput").ap()
    sgc = nc.dram_tensor("sgc", [128, HC], F32, kind="ExternalInput").ap()
    suc = nc.dram_tensor("suc", [128, HC], F32, kind="ExternalInput").ap()
    sdq = nc.dram_tensor("sdq", [SL, 128, D], F32, kind="ExternalInput").ap()
    dcol = nc.dram_tensor("dcol", [128, DC], F32, kind="ExternalInput").ap()
    ident = nc.dram_tensor("ident", [128, 128], BF16, kind="ExternalInput").ap()
    out = nc.dram_tensor("out", [TOK, D], F32, kind="ExternalOutput").ap()

    sg_img = nc.dram_tensor("sg_img", [HC, 128, DC, 128], BF16, kind="Internal").ap()
    su_img = nc.dram_tensor("su_img", [HC, 128, DC, 128], BF16, kind="Internal").ap()
    sb_img = nc.dram_tensor("sb_img", [NCH, 128, DC, 128], BF16, kind="Internal").ap()
    sc_img = nc.dram_tensor("sc_img", [SL, 128, NCH, 512], BF16, kind="Internal").ap()
    sd_img = nc.dram_tensor("sd_img", [SL * 11, 128, D], BF16, kind="Internal").ap()

    with ExitStack() as ctx:
        E = ctx.enter_context

        SGC = E(nc.sbuf_tensor("SGC", [128, HC], F32))
        SUC = E(nc.sbuf_tensor("SUC", [128, HC], F32))
        DCOL = E(nc.sbuf_tensor("DCOL", [128, DC], F32))
        IDENT = E(nc.sbuf_tensor("IDENT", [128, 128], BF16))
        DDIAG = E(nc.sbuf_tensor("DDIAG", [128, DC, 128], BF16))

        S1T = E(nc.sbuf_tensor("S1T", [128, 8], F32))
        S2T = E(nc.sbuf_tensor("S2T", [128, 8], F32))
        MEANT = E(nc.sbuf_tensor("MEANT", [128, 8], F32))
        MSQT = E(nc.sbuf_tensor("MSQT", [128, 8], F32))
        VART = E(nc.sbuf_tensor("VART", [128, 8], F32))
        STDT = E(nc.sbuf_tensor("STDT", [128, 8], F32))
        RSTDT = E(nc.sbuf_tensor("RSTDT", [128, 8], F32))
        S1PT = E(nc.sbuf_tensor("S1PT", [128, G, SL], F32))
        TR4 = E(nc.sbuf_tensor("TR4", [128, SL], F32))

        with ExitStack() as pctx:
            P_ = pctx.enter_context
            LQ = [P_(nc.sbuf_tensor(f"LQ{i}", [128, D], I32)) for i in range(10)]
            BF32 = [P_(nc.sbuf_tensor(f"BF32{i}", [128, 2 * D], F32)) for i in range(2)]
            WOUT = [P_(nc.sbuf_tensor(f"WOUT{i}", [128, 2 * D], BF16)) for i in range(8)]
            SDP = P_(nc.sbuf_tensor("SDP", [128, D], F32))

        XT = [E(nc.sbuf_tensor(f"XT{b}", [128, D], F32)) for b in range(NXB)]
        HTM = [E(nc.sbuf_tensor(f"HTM{i}", [128, D], BF16)) for i in range(2)]
        HFM1 = E(nc.sbuf_tensor("HFM1", [128, DC, T], BF16))
        HFM2 = E(nc.sbuf_tensor("HFM2", [128, DC, T], BF16))
        SFM = E(nc.sbuf_tensor("SFM", [128, NCH, T], BF16))
        GFM = E(nc.sbuf_tensor("GFM", [128, HC, T], BF16))
        WGU = [E(nc.sbuf_tensor(f"WGU{i}", [128, DC, 128], BF16)) for i in range(3)]
        WBB = [E(nc.sbuf_tensor(f"WBB{i}", [128, DC // 2, 128], BF16)) for i in range(4)]
        WCB = [E(nc.sbuf_tensor(f"WCB{i}", [128, NCH // 2, T], BF16)) for i in range(4)]
        WDB = [E(nc.sbuf_tensor(f"WDB{i}", [128, D], BF16)) for i in range(3)]
        TRA = E(nc.sbuf_tensor("TRA", [128, D], BF16))

        PS = E(nc.psum_tensor("PS", [128, 8, 512], F32))

        sem = {}
        for name in ["LDC", "LSDP", "P_DQ",
                     "MM_ST", "MM_AT", "MM_G", "MM_U", "MM_DN", "FR_D",
                     "EV_AT", "EV_DN", "TANH", "SILU", "GMUL",
                     "APPLY", "SQ_A", "FIN", "SQRT_A", "DDIAG_OK", "DS"]:
            sem[name] = E(nc.semaphore(name))
        PLD1 = [E(nc.semaphore(f"PLD1_{i}")) for i in range(4)]
        PLD2 = [E(nc.semaphore(f"PLD2_{i}")) for i in range(10)]
        PWR = [E(nc.semaphore(f"PWR{i}")) for i in range(8)]
        LDX = [E(nc.semaphore(f"LDX{i}")) for i in range(NXB)]
        STX = [E(nc.semaphore(f"STX{i}")) for i in range(NXB)]
        LWB = [E(nc.semaphore(f"LWB{i}")) for i in range(4)]
        LWC = [E(nc.semaphore(f"LWC{i}")) for i in range(4)]
        LWGU = [E(nc.semaphore(f"LWGU{i}")) for i in range(3)]
        LWD = [E(nc.semaphore(f"LWD{i}")) for i in range(3)]
        LWDS = [E(nc.semaphore(f"LWDS{i}")) for i in range(3)]
        TPS = [E(nc.semaphore(f"TPS{i}")) for i in range(4)]

        block = E(nc.Block())
        PWR_FINAL = [16 * len([1 for u in range(NUNITS) if u % 8 == i])
                     for i in range(8)]

        wd_info = {}
        _cnt_wd = {}
        for _p in range(n_pass):
            for _s in range(SL):
                for _j in range(11):
                    _u = HC * _p + 11 * _s + _j
                    special = _u >= 3 and (_u - 3) % 11 == 10
                    key = ("S" if special else "H", _u % 3)
                    _cnt_wd[key] = _cnt_wd.get(key, 0) + 16
                    wd_info[_u] = (special, _u % 3, _cnt_wd[key])

        AT_TILES = [(g, 2 * sp + si) for sp in range(2) for g in range(G)
                    for si in range(2)]
        AT_BANK = [2 * (s % 2) + (g % 2) for (g, s) in AT_TILES]

        def xb(p, g):
            return (4 * p + g) % NXB

        def ldx_val(p, g):
            return 16 * ((4 * p + g) // NXB + 1)

        @block.sync
        def _(sync):
            for apsrc, sb in [(sgc, SGC), (suc, SUC),
                              (dcol, DCOL), (ident, IDENT)]:
                sync.dma_start(out=sb[:], in_=apsrc).then_inc(sem["LDC"], 16)

            def emit_prep_load(u):
                kind, i = unit_kind(u)
                if kind == "D" and i % 11 == 0:
                    s_ = i // 11
                    if s_ > 0:
                        sync.wait_ge(sem["P_DQ"], NU_B + NU_C + NU_G + NU_U + 11 * s_)
                    sync.dma_start(out=SDP[:], in_=sdq[s_]).then_inc(sem["LSDP"], 16)
                if kind in ("B", "C"):
                    if u >= 2:
                        sync.wait_ge(sem["P_DQ"], u - 1)
                    w = D if kind == "B" else NCH * 512
                    src = bmi[i] if kind == "B" else cmi[i]
                    sync.dma_start(out=BF32[u % 2][:, :w], in_=src).then_inc(PLD1[u % 2], 16)
                else:
                    o = u - (NU_B + NU_C)
                    if o >= 10:
                        sync.wait_ge(sem["P_DQ"], u - 10 + 1)
                    src = wgi[i] if kind == "G" else (wui[i] if kind == "U" else wdi[i])
                    sync.dma_start(out=LQ[o % 10][:], in_=src).then_inc(PLD2[o % 10], 16)

            for u in range(NUNITS):
                emit_prep_load(u)

            for i in range(8):
                sync.wait_ge(PWR[i], PWR_FINAL[i])

            def emit_b(p, half, n, hh):
                u = 16 * p + 8 * half + n
                h = 2 * u + hh
                if h >= 4:
                    sync.wait_ge(sem["MM_ST"], (h - 4) // 2 + 1)
                sync.dma_start(out=WBB[h % 4][:],
                               in_=sb_img[n][:, hh * 8:(hh + 1) * 8, :]
                               ).then_inc(LWB[h % 4], 16)

            def emit_c(p, sp, si, h2_):
                s = 2 * sp + si
                slot = 2 * si + h2_
                if p > 0 or sp > 0:
                    pm = (16 * p + 8 * (sp - 1) + 7 + si) if sp > 0 \
                        else (16 * (p - 1) + 15 + si)
                    sync.wait_ge(sem["MM_AT"], pm)
                sync.dma_start(out=WCB[slot][:],
                               in_=sc_img[s][:, h2_ * 4:(h2_ + 1) * 4, :]
                               ).then_inc(LWC[slot], 16)

            def emit_gu(p, hc, is_up):
                t = 2 * (HC * p + hc) + is_up
                if t >= 3:
                    hc3 = (t - 3) // 2
                    if (t - 3) % 2 == 0:
                        sync.wait_ge(sem["MM_G"], hc3 + 1)
                    else:
                        sync.wait_ge(sem["MM_U"], hc3 + 1)
                sync.dma_start(out=WGU[t % 3][:],
                               in_=(su_img if is_up else sg_img)[hc]
                               ).then_inc(LWGU[t % 3], 16)

            for p in range(n_pass):
                for half in range(2):
                    for n in range(NCH):
                        for hh in range(2):
                            if p > 0 and half == 0 and 2 * n + hh < 4:
                                continue
                            emit_b(p, half, n, hh)
                for sp in range(2):
                    for si in range(2):
                        for h2_ in range(2):
                            if p > 0 and sp == 0:
                                continue
                            emit_c(p, sp, si, h2_)
                for hc in range(HC):
                    for is_up in range(2):
                        if p > 0 and 2 * hc + is_up < 3:
                            continue
                        emit_gu(p, hc, is_up)
                if p + 1 < n_pass:
                    for q in range(4):
                        emit_b(p + 1, 0, q // 2, q % 2)
                    for si in range(2):
                        for h2_ in range(2):
                            emit_c(p + 1, 0, si, h2_)
                    for q in range(3):
                        emit_gu(p + 1, q // 2, q % 2)
                for s in range(SL):
                    for j in range(11):
                        u = HC * p + 11 * s + j
                        if wd_info[u][0]:
                            continue
                        if u >= 3:
                            sync.wait_ge(sem["FR_D"], u - 2)
                        sync.dma_start(out=WDB[u % 3][:],
                                       in_=sd_img[s * 11 + j]).then_inc(LWD[u % 3], 16)
            for b in range(NXB):
                n_st = len([1 for gi in range(4 * n_pass) if gi % NXB == b])
                if n_st:
                    sync.wait_ge(STX[b], 16 * n_st)

        @block.gpsimd
        def _(gpsimd):
            for u in range(NUNITS):
                kind, i = unit_kind(u)
                gpsimd.wait_ge(sem["P_DQ"], u + 1)
                if kind == "B":
                    dst, w = sb_img[i], D
                elif kind == "C":
                    dst, w = sc_img[i], NCH * 512
                elif kind == "G":
                    dst, w = sg_img[i], D
                elif kind == "U":
                    dst, w = su_img[i], D
                else:
                    dst, w = sd_img[i], D
                gpsimd.dma_start(out=dst, in_=WOUT[u % 8][:, :w]).then_inc(PWR[u % 8], 16)

            for p in range(n_pass):
                for s in range(SL):
                    for j in range(11):
                        u = HC * p + 11 * s + j
                        if not wd_info[u][0]:
                            continue
                        gpsimd.wait_ge(sem["FR_D"], u - 2)
                        pu = u - 3
                        pp, ps = pu // 44, (pu % 44) // 11
                        gpsimd.wait_ge(sem["MM_DN"], 4 * pp + ps + 1)
                        gpsimd.dma_start(out=WDB[u % 3][:],
                                         in_=sd_img[s * 11 + j]).then_inc(LWDS[u % 3], 16)

        @block.scalar
        def _(scalar):
            for i in range(8):
                scalar.wait_ge(PWR[i], PWR_FINAL[i])
            scalar.wait_ge(sem["LDC"], 64)

            def emit_xload(pp, g):
                gi = 4 * pp + g
                b = gi % NXB
                if gi >= NXB:
                    scalar.wait_ge(STX[b], 16 * (gi // NXB))
                scalar.dma_start(out=XT[b][:],
                                 in_=xin[gi * 128:(gi + 1) * 128, :]).then_inc(LDX[b], 16)

            def act_ln1(pp, groups):
                for g in groups:
                    k = 8 * pp + g
                    scalar.wait_ge(LDX[xb(pp, g)], ldx_val(pp, g))
                    if k >= 1:
                        scalar.wait_ge(sem["SQ_A"], k)
                    scalar.activation(TRA[:], XT[xb(pp, g)][:], AF.Square,
                                      accum_out=S2T[:, g:g + 1]).then_inc(sem["SQ_A"], 1)
                for g in groups:
                    k = 8 * pp + g
                    scalar.wait_ge(sem["FIN"], k + 1)
                    scalar.activation(STDT[:, g:g + 1], VART[:, g:g + 1],
                                      AF.Sqrt).then_inc(sem["SQRT_A"], 1)
                    scalar.wait_ge(sem["APPLY"], k + 1)
                    if pp > 0:
                        scalar.wait_ge(sem["MM_AT"], 16 * pp)
                    scalar.dma_start(out=HFM1[:, :, g * 128:(g + 1) * 128],
                                     in_=HTM[k % 2][:], transpose=True).then_inc(TPS[k % 4], 16)

            for g in range(G):
                emit_xload(0, g)
            act_ln1(0, (0, 1, 2, 3))
            for p in range(n_pass):
                if p + 1 < n_pass:
                    emit_xload(p + 1, 0)
                    emit_xload(p + 1, 1)
                for n in range(NCH):
                    if n == 0 and p > 0:
                        scalar.wait_ge(sem["MM_AT"], 16 * p)
                    scalar.wait_ge(sem["MM_ST"], 16 * p + 8 + n + 1)
                    scalar.activation(SFM[:, n, :], PS[:, n, :],
                                      AF.Tanh).then_inc(sem["TANH"], 1)
                for g in range(G):
                    k = 8 * p + 4 + g
                    scalar.wait_ge(sem["EV_AT"], 16 * p + 2 * g + 10)
                    scalar.wait_ge(sem["SQ_A"], k)
                    scalar.activation(TRA[:], XT[xb(p, g)][:], AF.Square,
                                      accum_out=S2T[:, 4 + g:5 + g]).then_inc(sem["SQ_A"], 1)
                    scalar.wait_ge(sem["FIN"], k + 1)
                    scalar.activation(STDT[:, 4 + g:5 + g], VART[:, 4 + g:5 + g],
                                      AF.Sqrt).then_inc(sem["SQRT_A"], 1)
                    scalar.wait_ge(sem["APPLY"], k + 1)
                    if p > 0:
                        scalar.wait_ge(sem["MM_U"], HC * p)
                    scalar.dma_start(out=HFM2[:, :, g * 128:(g + 1) * 128],
                                     in_=HTM[k % 2][:], transpose=True).then_inc(TPS[k % 4], 16)
                for hc in range(HC):
                    if hc == 0 and p > 0:
                        scalar.wait_ge(sem["MM_DN"], 4 * p)
                    scalar.wait_ge(sem["MM_G"], HC * p + hc + 1)
                    scalar.activation(GFM[:, hc, :], PS[:, 4 + hc % 2, :],
                                      AF.Silu, scale=SGC[:, hc:hc + 1]).then_inc(sem["SILU"], 1)
                if p + 1 < n_pass:
                    act_ln1(p + 1, (0, 1))
                for g in (0, 1):
                    scalar.wait_ge(sem["EV_DN"], 16 * p + 13 + g)
                    scalar.dma_start(out=out[(4 * p + g) * 128:(4 * p + g + 1) * 128, :],
                                     in_=XT[xb(p, g)][:]).then_inc(STX[xb(p, g)], 16)
                if p + 1 < n_pass:
                    emit_xload(p + 1, 2)
                    emit_xload(p + 1, 3)
                    act_ln1(p + 1, (2, 3))
                for g in (2, 3):
                    scalar.wait_ge(sem["EV_DN"], 16 * p + 13 + g)
                    scalar.dma_start(out=out[(4 * p + g) * 128:(4 * p + g + 1) * 128, :],
                                     in_=XT[xb(p, g)][:]).then_inc(STX[xb(p, g)], 16)

        @block.vector
        def _(vector):
            for u in range(NUNITS):
                kind, i = unit_kind(u)
                if u >= 8:
                    vector.wait_ge(PWR[u % 8], 16 * (u // 8))
                if kind in ("B", "C"):
                    w = D if kind == "B" else NCH * 512
                    vector.wait_ge(PLD1[u % 2], 16 * (u // 2 + 1))
                    vector.tensor_copy(WOUT[u % 8][:, :w],
                                       BF32[u % 2][:, :w]).then_inc(sem["P_DQ"], 1)
                elif kind in ("G", "U"):
                    o = u - (NU_B + NU_C)
                    vector.wait_ge(PLD2[o % 10], 16 * (o // 10 + 1))
                    vector.tensor_scalar_add(WOUT[u % 8][:, :D],
                                             LQ[o % 10][:], -128.0).then_inc(sem["P_DQ"], 1)
                else:
                    o = u - (NU_B + NU_C)
                    vector.wait_ge(PLD2[o % 10], 16 * (o // 10 + 1))
                    vector.wait_ge(sem["LSDP"], 16 * (i // 11 + 1))
                    vector.scalar_tensor_tensor(WOUT[u % 8][:, :D], LQ[o % 10][:],
                                                -128.0, SDP[:], ALU.add,
                                                ALU.mult).then_inc(sem["P_DQ"], 1)

            for i in range(8):
                vector.wait_ge(PWR[i], PWR_FINAL[i])
            vector.wait_ge(sem["LDC"], 64)
            ins = None
            for c in range(DC):
                ins = vector.tensor_scalar_mul(DDIAG[:, c, :], IDENT[:], DCOL[:, c:c + 1])
            ins.then_inc(sem["DDIAG_OK"], 1)

            dsc = {"n": 0}

            def ds_inc(ins):
                dsc["n"] += 1
                ins.then_inc(sem["DS"], 1)

            def ds_wait():
                vector.wait_ge(sem["DS"], dsc["n"])

            def ln_finalize(k, i):
                vector.wait_ge(sem["SQ_A"], k + 1)
                ds_wait()
                ds_inc(vector.tensor_scalar_mul(MEANT[:, i:i + 1], S1T[:, i:i + 1], 1.0 / D))
                ds_wait()
                ds_inc(vector.tensor_scalar(MSQT[:, i:i + 1], MEANT[:, i:i + 1],
                                            MEANT[:, i:i + 1], -EPS, ALU.mult, ALU.add))
                ds_wait()
                vector.scalar_tensor_tensor(VART[:, i:i + 1], S2T[:, i:i + 1], 1.0 / D,
                                            MSQT[:, i:i + 1], ALU.mult,
                                            ALU.subtract).then_inc(sem["FIN"], 1)
                vector.wait_ge(sem["SQRT_A"], k + 1)
                ds_inc(vector.reciprocal(RSTDT[:, i:i + 1], STDT[:, i:i + 1]))

            def htm_free(k):
                if k >= 2:
                    vector.wait_ge(TPS[(k - 2) % 4], 16 * ((k - 2) // 4 + 1))

            def ln_apply(k, i, b):
                ds_wait()
                vector.tensor_scalar(HTM[k % 2][:], XT[b][:], MEANT[:, i:i + 1],
                                     RSTDT[:, i:i + 1], ALU.subtract,
                                     ALU.mult).then_inc(sem["APPLY"], 1)

            def dve_ln1(pp, groups):
                for g in groups:
                    k = 8 * pp + g
                    vector.wait_ge(LDX[xb(pp, g)], ldx_val(pp, g))
                    htm_free(k)
                    ds_inc(vector.tensor_scalar(HTM[k % 2][:], XT[xb(pp, g)][:], 1.0, 0.0,
                                                ALU.mult, ALU.add,
                                                accum_out=S1T[:, g:g + 1]))
                    ln_finalize(k, g)
                    ln_apply(k, g, xb(pp, g))

            dve_ln1(0, (0, 1, 2, 3))
            for p in range(n_pass):
                def at_evac(m):
                    g, s = AT_TILES[m]
                    vector.wait_ge(sem["MM_AT"], 16 * p + m + 1)
                    vector.scalar_tensor_tensor(
                        XT[xb(p, g)][:, s * 512:(s + 1) * 512], PS[:, AT_BANK[m], :],
                        1.0, XT[xb(p, g)][:, s * 512:(s + 1) * 512], ALU.mult, ALU.add,
                        accum_out=S1PT[:, g, s:s + 1]).then_inc(sem["EV_AT"], 1)

                for m in range(8):
                    at_evac(m)
                for g in range(G):
                    at_evac(8 + 2 * g)
                    at_evac(8 + 2 * g + 1)
                    k = 8 * p + 4 + g
                    vector.wait_ge(sem["EV_AT"], 16 * p + 2 * g + 10)
                    htm_free(k)
                    ds_inc(vector.tensor_scalar(TR4[:], S1PT[:, g, :], 1.0, 0.0, ALU.mult,
                                                ALU.add, accum_out=S1T[:, 4 + g:5 + g]))
                    ln_finalize(k, 4 + g)
                    ln_apply(k, 4 + g, xb(p, g))
                for hc in range(HC):
                    vector.wait_ge(sem["SILU"], HC * p + hc + 1)
                    vector.wait_ge(sem["MM_U"], HC * p + hc + 1)
                    vector.scalar_tensor_tensor(GFM[:, hc, :], PS[:, 6 + hc % 2, :],
                                                SUC[:, hc:hc + 1], GFM[:, hc, :],
                                                ALU.mult, ALU.mult).then_inc(sem["GMUL"], 1)
                if p + 1 < n_pass:
                    dve_ln1(p + 1, (0, 1))
                for s in range(SL):
                    for g in range(G):
                        vector.wait_ge(sem["MM_DN"], 4 * p + s + 1)
                        vector.tensor_tensor(XT[xb(p, g)][:, s * 512:(s + 1) * 512],
                                             XT[xb(p, g)][:, s * 512:(s + 1) * 512],
                                             PS[:, (s % 2) * 4 + g, :],
                                             ALU.add).then_inc(sem["EV_DN"], 1)
                if p + 1 < n_pass:
                    dve_ln1(p + 1, (2, 3))

        @block.tensor
        def _(tensor):
            tensor.wait_ge(sem["DDIAG_OK"], 1)
            for p in range(n_pass):
                for half in range(2):
                    lo, hi = half * 256, half * 256 + 256
                    for n in range(NCH):
                        if n == 0:
                            for i in ((0, 1) if half == 0 else (2, 3)):
                                tensor.wait_ge(TPS[i], 16 * (2 * p + 1))
                        u = 16 * p + 8 * half + n
                        if half == 0 and p > 0:
                            tensor.wait_ge(sem["EV_DN"], 16 * (p - 1) + 9 + n)
                        ins = None
                        for c in range(DC):
                            if c % 8 == 0:
                                h = 2 * u + c // 8
                                tensor.wait_ge(LWB[h % 4], 16 * (h // 4 + 1))
                            ins = tensor.matmul(PS[:, n, lo:hi],
                                                WBB[(2 * u + c // 8) % 4][:, c % 8, :],
                                                HFM1[:, c, lo:hi],
                                                start=(c == 0), stop=(c == DC - 1))
                        ins.then_inc(sem["MM_ST"], 1)
                for m in range(16):
                    g, s = AT_TILES[m]
                    bank = AT_BANK[m]
                    if m == 0:
                        tensor.wait_ge(sem["TANH"], 8 * p + 8)
                    if 16 * p + m - 3 >= 1:
                        tensor.wait_ge(sem["EV_AT"], 16 * p + m - 3)
                    for sub in range(4):
                        dcg = s * 4 + sub
                        tensor.matmul(PS[:, bank, sub * 128:(sub + 1) * 128],
                                      HFM1[:, dcg, g * 128:(g + 1) * 128],
                                      DDIAG[:, dcg, :],
                                      start=(sub == 0), stop=False)
                    ins = None
                    for n in range(NCH):
                        if n % 4 == 0:
                            tensor.wait_ge(LWC[2 * (s % 2) + n // 4],
                                           16 * (2 * p + s // 2 + 1))
                        ins = tensor.matmul(PS[:, bank, :],
                                            SFM[:, n, g * 128:(g + 1) * 128],
                                            WCB[2 * (s % 2) + n // 4][:, n % 4, :],
                                            start=False, stop=(n == NCH - 1))
                    ins.then_inc(sem["MM_AT"], 1)

                def gate_chain(hc, lo, hi, inc):
                    t = 2 * (HC * p + hc)
                    ins = None
                    for c in range(DC):
                        ins = tensor.matmul(PS[:, 4 + hc % 2, lo:hi], WGU[t % 3][:, c, :],
                                            HFM2[:, c, lo:hi],
                                            start=(c == 0), stop=(c == DC - 1))
                    if inc:
                        ins.then_inc(sem["MM_G"], 1)

                def up_chain(hc, lo, hi, inc):
                    t = 2 * (HC * p + hc) + 1
                    ins = None
                    for c in range(DC):
                        ins = tensor.matmul(PS[:, 6 + hc % 2, lo:hi], WGU[t % 3][:, c, :],
                                            HFM2[:, c, lo:hi],
                                            start=(c == 0), stop=(c == DC - 1))
                    if inc:
                        ins.then_inc(sem["MM_U"], 1)

                for hc in (0, 1):
                    if hc == 0:
                        tensor.wait_ge(TPS[0], 16 * (2 * p + 2))
                        tensor.wait_ge(TPS[1], 16 * (2 * p + 2))
                    t = 2 * (HC * p + hc)
                    tensor.wait_ge(LWGU[t % 3], 16 * (t // 3 + 1))
                    if p > 0:
                        tensor.wait_ge(sem["EV_DN"], 16 * (p - 1) + 13 + hc)
                    gate_chain(hc, 0, 256, False)
                for hc in (0, 1):
                    if hc == 0:
                        tensor.wait_ge(TPS[2], 16 * (2 * p + 2))
                        tensor.wait_ge(TPS[3], 16 * (2 * p + 2))
                    gate_chain(hc, 256, 512, True)
                for hc in (0, 1):
                    t = 2 * (HC * p + hc) + 1
                    tensor.wait_ge(LWGU[t % 3], 16 * (t // 3 + 1))
                    if p > 0:
                        tensor.wait_ge(sem["EV_DN"], 16 * (p - 1) + 15 + hc)
                    up_chain(hc, 0, 512, True)
                for hc in range(2, HC):
                    t = 2 * (HC * p + hc)
                    tensor.wait_ge(LWGU[t % 3], 16 * (t // 3 + 1))
                    tensor.wait_ge(sem["SILU"], HC * p + hc - 1)
                    gate_chain(hc, 0, 512, True)
                    tensor.wait_ge(LWGU[(t + 1) % 3], 16 * ((t + 1) // 3 + 1))
                    tensor.wait_ge(sem["GMUL"], HC * p + hc - 1)
                    up_chain(hc, 0, 512, True)

                for s in range(SL):
                    for j in range(11):
                        u = HC * p + 11 * s + j
                        spec, slot, val = wd_info[u]
                        tensor.wait_ge(LWDS[slot] if spec else LWD[slot], val)
                        tensor.wait_ge(sem["GMUL"], HC * p + 4 * j + 4)
                        for q in range(4):
                            hc = 4 * j + q
                            for g in range(G):
                                bank = (s % 2) * 4 + g
                                if hc == 0:
                                    if s >= 2:
                                        tensor.wait_ge(sem["EV_DN"], 16 * p + 4 * (s - 2) + g + 1)
                                    elif s == 0:
                                        tensor.wait_ge(sem["EV_AT"],
                                                       16 * p + 13 + (2, 3, 0, 1)[g])
                                    else:
                                        if g < 2:
                                            tensor.wait_ge(sem["SILU"], HC * p + 43 + g)
                                        else:
                                            tensor.wait_ge(sem["GMUL"], HC * p + 41 + g)
                                ins = tensor.matmul(PS[:, bank, :],
                                                    GFM[:, hc, g * 128:(g + 1) * 128],
                                                    WDB[u % 3][:, q * 512:(q + 1) * 512],
                                                    start=(hc == 0), stop=(hc == HC - 1))
                                if hc == HC - 1 and g == G - 1:
                                    ins.then_inc(sem["MM_DN"], 1)
                                elif hc == HC - 1 and g == G - 2:
                                    ins.then_inc(sem["FR_D"], 1)
                        if j < 10:
                            ins.then_inc(sem["FR_D"], 1)

        return nc


def kernel(L0_attn_x, L0_attn_B, L0_attn_C, L0_attn_D,
           L0_ffn_wq_gate, L0_ffn_s_gate, L0_ffn_wq_up, L0_ffn_s_up,
           L0_ffn_wq_down, L0_ffn_s_down):
    x = np.asarray(L0_attn_x, dtype=np.float32).reshape(NCORE * TOK, D)
    B = np.ascontiguousarray(np.asarray(L0_attn_B, dtype=np.float32))
    C = np.ascontiguousarray(np.asarray(L0_attn_C, dtype=np.float32))
    Dv = np.asarray(L0_attn_D, dtype=np.float32)
    wg = np.ascontiguousarray(np.asarray(L0_ffn_wq_gate, dtype=np.int32))
    wu = np.ascontiguousarray(np.asarray(L0_ffn_wq_up, dtype=np.int32))
    wd = np.ascontiguousarray(np.asarray(L0_ffn_wq_down, dtype=np.int32))
    sg = np.asarray(L0_ffn_s_gate, dtype=np.float32)
    su = np.asarray(L0_ffn_s_up, dtype=np.float32)
    sd = np.asarray(L0_ffn_s_down, dtype=np.float32)

    sgc = np.ascontiguousarray(sg.reshape(HC, 128).T)
    suc = np.ascontiguousarray(su.reshape(HC, 128).T)
    sdq = np.ascontiguousarray(
        np.broadcast_to(sd.reshape(SL, 1, 1, 512), (SL, 128, 4, 512)).reshape(SL, 128, D))
    dcol = np.ascontiguousarray(Dv.reshape(DC, 128).T)
    ident = np.eye(128, dtype=np.float32).astype(ml_dtypes.bfloat16)

    wgi = np.ascontiguousarray(
        wg.reshape(HC, 128, DC, 128).transpose(0, 3, 2, 1).reshape(HC, 128, DC * 128))
    wui = np.ascontiguousarray(
        wu.reshape(HC, 128, DC, 128).transpose(0, 3, 2, 1).reshape(HC, 128, DC * 128))
    wdi = np.ascontiguousarray(
        wd.reshape(SL, 512, 11, 4, 128).transpose(0, 2, 4, 3, 1).reshape(SL * 11, 128, D))
    bmi = np.ascontiguousarray(
        B.reshape(NCH, 128, DC, 128).transpose(0, 3, 2, 1).reshape(NCH, 128, DC * 128))
    cmi = np.ascontiguousarray(
        C.reshape(SL, 512, NCH, 128).transpose(0, 3, 2, 1).reshape(SL, 128, NCH * 512))

    nc = build_nc(TOK // T)
    common = dict(wgi=wgi, wui=wui, wdi=wdi, bmi=bmi, cmi=cmi, sgc=sgc, suc=suc,
                  sdq=sdq, dcol=dcol, ident=ident)
    in_maps = [dict(xin=np.ascontiguousarray(x[i * TOK:(i + 1) * TOK]), **common)
               for i in range(NCORE)]
    res = run_bass_kernel_spmd(nc, in_maps, core_ids=list(range(NCORE)))
    outs = [res.results[i]["out"] for i in range(NCORE)]
    return np.concatenate(outs, axis=0).reshape(16, 2048, 2048).astype(np.float32)


# version 19
# speedup vs baseline: 1.0079x; 1.0079x over previous
import numpy as np
import ml_dtypes
import concourse.bass as bass
import concourse.mybir as mybir
from concourse.bass_utils import run_bass_kernel_spmd
from contextlib import ExitStack

F32 = mybir.dt.float32
BF16 = mybir.dt.bfloat16
I32 = mybir.dt.int32
AF = mybir.ActivationFunctionType
ALU = mybir.AluOpType

NCORE = 8
TOK = 4096
D = 2048
NST = 1024
HID = 5632
T = 512
G = 4
DC = D // 128
NCH = NST // 128
HC = HID // 128
SL = 4
NXB = 6
EPS = 1e-5

NU_B, NU_C, NU_G, NU_U, NU_D = NCH, SL, HC, HC, HC
NUNITS = NU_B + NU_C + NU_G + NU_U + NU_D


def unit_kind(u):
    if u < NU_B:
        return ("B", u)
    u -= NU_B
    if u < NU_C:
        return ("C", u)
    u -= NU_C
    if u < NU_G:
        return ("G", u)
    u -= NU_G
    if u < NU_U:
        return ("U", u)
    return ("D", u - NU_U)


def build_nc(n_pass):
    nc = bass.Bass("TRN2", target_bir_lowering=False)

    xin = nc.dram_tensor("xin", [TOK, D], F32, kind="ExternalInput").ap()
    wgi = nc.dram_tensor("wgi", [HC, 128, DC * 128], I32, kind="ExternalInput").ap()
    wui = nc.dram_tensor("wui", [HC, 128, DC * 128], I32, kind="ExternalInput").ap()
    wdi = nc.dram_tensor("wdi", [SL * 11, 128, D], I32, kind="ExternalInput").ap()
    bmi = nc.dram_tensor("bmi", [NCH, 128, DC * 128], F32, kind="ExternalInput").ap()
    cmi = nc.dram_tensor("cmi", [SL, 128, NCH * 512], F32, kind="ExternalInput").ap()
    sgc = nc.dram_tensor("sgc", [128, HC], F32, kind="ExternalInput").ap()
    suc = nc.dram_tensor("suc", [128, HC], F32, kind="ExternalInput").ap()
    sdq = nc.dram_tensor("sdq", [SL, 128, D], F32, kind="ExternalInput").ap()
    dcol = nc.dram_tensor("dcol", [128, DC], F32, kind="ExternalInput").ap()
    ident = nc.dram_tensor("ident", [128, 128], BF16, kind="ExternalInput").ap()
    out = nc.dram_tensor("out", [TOK, D], F32, kind="ExternalOutput").ap()

    sg_img = nc.dram_tensor("sg_img", [HC, 128, DC, 128], BF16, kind="Internal").ap()
    su_img = nc.dram_tensor("su_img", [HC, 128, DC, 128], BF16, kind="Internal").ap()
    sb_img = nc.dram_tensor("sb_img", [NCH, 128, DC, 128], BF16, kind="Internal").ap()
    sc_img = nc.dram_tensor("sc_img", [SL, 128, NCH, 512], BF16, kind="Internal").ap()
    sd_img = nc.dram_tensor("sd_img", [SL * 11, 128, D], BF16, kind="Internal").ap()

    with ExitStack() as ctx:
        E = ctx.enter_context

        SGC = E(nc.sbuf_tensor("SGC", [128, HC], F32))
        SUC = E(nc.sbuf_tensor("SUC", [128, HC], F32))
        DCOL = E(nc.sbuf_tensor("DCOL", [128, DC], F32))
        IDENT = E(nc.sbuf_tensor("IDENT", [128, 128], BF16))
        DDIAG = E(nc.sbuf_tensor("DDIAG", [128, DC, 128], BF16))

        S1T = E(nc.sbuf_tensor("S1T", [128, 8], F32))
        S2T = E(nc.sbuf_tensor("S2T", [128, 8], F32))
        MEANT = E(nc.sbuf_tensor("MEANT", [128, 8], F32))
        MSQT = E(nc.sbuf_tensor("MSQT", [128, 8], F32))
        VART = E(nc.sbuf_tensor("VART", [128, 8], F32))
        STDT = E(nc.sbuf_tensor("STDT", [128, 8], F32))
        RSTDT = E(nc.sbuf_tensor("RSTDT", [128, 8], F32))
        S1PT = E(nc.sbuf_tensor("S1PT", [128, G, SL], F32))
        TR4 = E(nc.sbuf_tensor("TR4", [128, SL], F32))

        with ExitStack() as pctx:
            P_ = pctx.enter_context
            LQ = [P_(nc.sbuf_tensor(f"LQ{i}", [128, D], I32)) for i in range(10)]
            BF32 = [P_(nc.sbuf_tensor(f"BF32{i}", [128, 2 * D], F32)) for i in range(2)]
            WOUT = [P_(nc.sbuf_tensor(f"WOUT{i}", [128, 2 * D], BF16)) for i in range(8)]
            SDP = P_(nc.sbuf_tensor("SDP", [128, D], F32))

        XT = [E(nc.sbuf_tensor(f"XT{b}", [128, D], F32)) for b in range(NXB)]
        HTM = [E(nc.sbuf_tensor(f"HTM{i}", [128, D], BF16)) for i in range(2)]
        HFM1 = E(nc.sbuf_tensor("HFM1", [128, DC, T], BF16))
        HFM2 = E(nc.sbuf_tensor("HFM2", [128, DC, T], BF16))
        SFM = E(nc.sbuf_tensor("SFM", [128, NCH, T], BF16))
        GFM = E(nc.sbuf_tensor("GFM", [128, HC, T], BF16))
        WGU = [E(nc.sbuf_tensor(f"WGU{i}", [128, DC, 128], BF16)) for i in range(3)]
        WBB = [E(nc.sbuf_tensor(f"WBB{i}", [128, DC // 2, 128], BF16)) for i in range(6)]
        WCB = [E(nc.sbuf_tensor(f"WCB{i}", [128, NCH // 2, T], BF16)) for i in range(4)]
        WDB = [E(nc.sbuf_tensor(f"WDB{i}", [128, D], BF16)) for i in range(3)]
        TRA = E(nc.sbuf_tensor("TRA", [128, D], BF16))

        PS = E(nc.psum_tensor("PS", [128, 8, 512], F32))

        sem = {}
        for name in ["LDC", "LSDP", "P_DQ",
                     "MM_ST", "MM_AT", "MM_G", "MM_U", "MM_DN", "FR_D",
                     "EV_AT", "EV_DN", "TANH", "SILU", "GMUL",
                     "APPLY", "SQ_A", "FIN", "SQRT_A", "DDIAG_OK", "DS"]:
            sem[name] = E(nc.semaphore(name))
        PLD1 = [E(nc.semaphore(f"PLD1_{i}")) for i in range(4)]
        PLD2 = [E(nc.semaphore(f"PLD2_{i}")) for i in range(10)]
        PWR = [E(nc.semaphore(f"PWR{i}")) for i in range(8)]
        LDX = [E(nc.semaphore(f"LDX{i}")) for i in range(NXB)]
        STX = [E(nc.semaphore(f"STX{i}")) for i in range(NXB)]
        LWB = [E(nc.semaphore(f"LWB{i}")) for i in range(6)]
        LWC = [E(nc.semaphore(f"LWC{i}")) for i in range(4)]
        LWGU = [E(nc.semaphore(f"LWGU{i}")) for i in range(3)]
        LWD = [E(nc.semaphore(f"LWD{i}")) for i in range(3)]
        LWDS = [E(nc.semaphore(f"LWDS{i}")) for i in range(3)]
        TPS = [E(nc.semaphore(f"TPS{i}")) for i in range(4)]

        block = E(nc.Block())
        PWR_FINAL = [16 * len([1 for u in range(NUNITS) if u % 8 == i])
                     for i in range(8)]

        wd_info = {}
        _cnt_wd = {}
        for _p in range(n_pass):
            for _s in range(SL):
                for _j in range(11):
                    _u = HC * _p + 11 * _s + _j
                    special = _u >= 3 and (_u - 3) % 11 == 10
                    key = ("S" if special else "H", _u % 3)
                    _cnt_wd[key] = _cnt_wd.get(key, 0) + 16
                    wd_info[_u] = (special, _u % 3, _cnt_wd[key])

        AT_TILES = [(g, 2 * sp + si) for sp in range(2) for g in range(G)
                    for si in range(2)]
        AT_BANK = [2 * (s % 2) + (g % 2) for (g, s) in AT_TILES]

        def xb(p, g):
            return (4 * p + g) % NXB

        def ldx_val(p, g):
            return 16 * ((4 * p + g) // NXB + 1)

        @block.sync
        def _(sync):
            for apsrc, sb in [(sgc, SGC), (suc, SUC),
                              (dcol, DCOL), (ident, IDENT)]:
                sync.dma_start(out=sb[:], in_=apsrc).then_inc(sem["LDC"], 16)

            def emit_prep_load(u):
                kind, i = unit_kind(u)
                if kind == "D" and i % 11 == 0:
                    s_ = i // 11
                    if s_ > 0:
                        sync.wait_ge(sem["P_DQ"], NU_B + NU_C + NU_G + NU_U + 11 * s_)
                    sync.dma_start(out=SDP[:], in_=sdq[s_]).then_inc(sem["LSDP"], 16)
                if kind in ("B", "C"):
                    if u >= 2:
                        sync.wait_ge(sem["P_DQ"], u - 1)
                    w = D if kind == "B" else NCH * 512
                    src = bmi[i] if kind == "B" else cmi[i]
                    sync.dma_start(out=BF32[u % 2][:, :w], in_=src).then_inc(PLD1[u % 2], 16)
                else:
                    o = u - (NU_B + NU_C)
                    if o >= 10:
                        sync.wait_ge(sem["P_DQ"], u - 10 + 1)
                    src = wgi[i] if kind == "G" else (wui[i] if kind == "U" else wdi[i])
                    sync.dma_start(out=LQ[o % 10][:], in_=src).then_inc(PLD2[o % 10], 16)

            for u in range(NUNITS):
                emit_prep_load(u)

            for i in range(8):
                sync.wait_ge(PWR[i], PWR_FINAL[i])

            def emit_b(p, half, n, hh):
                u = 16 * p + 8 * half + n
                h = 2 * u + hh
                if h >= 6:
                    sync.wait_ge(sem["MM_ST"], (h - 6) // 2 + 1)
                sync.dma_start(out=WBB[h % 6][:],
                               in_=sb_img[n][:, hh * 8:(hh + 1) * 8, :]
                               ).then_inc(LWB[h % 6], 16)

            def emit_c(p, sp, si, h2_):
                s = 2 * sp + si
                slot = 2 * si + h2_
                if p > 0 or sp > 0:
                    pm = (16 * p + 8 * (sp - 1) + 7 + si) if sp > 0 \
                        else (16 * (p - 1) + 15 + si)
                    sync.wait_ge(sem["MM_AT"], pm)
                sync.dma_start(out=WCB[slot][:],
                               in_=sc_img[s][:, h2_ * 4:(h2_ + 1) * 4, :]
                               ).then_inc(LWC[slot], 16)

            def emit_gu(p, hc, is_up):
                t = 2 * (HC * p + hc) + is_up
                if t >= 3:
                    hc3 = (t - 3) // 2
                    if (t - 3) % 2 == 0:
                        sync.wait_ge(sem["MM_G"], hc3 + 1)
                    else:
                        sync.wait_ge(sem["MM_U"], hc3 + 1)
                sync.dma_start(out=WGU[t % 3][:],
                               in_=(su_img if is_up else sg_img)[hc]
                               ).then_inc(LWGU[t % 3], 16)

            for p in range(n_pass):
                for half in range(2):
                    for n in range(NCH):
                        for hh in range(2):
                            if p > 0 and half == 0 and 2 * n + hh < 6:
                                continue
                            emit_b(p, half, n, hh)
                for sp in range(2):
                    for si in range(2):
                        for h2_ in range(2):
                            if p > 0 and sp == 0:
                                continue
                            emit_c(p, sp, si, h2_)
                for hc in range(HC):
                    for is_up in range(2):
                        if p > 0 and 2 * hc + is_up < 3:
                            continue
                        emit_gu(p, hc, is_up)
                if p + 1 < n_pass:
                    for q in range(6):
                        emit_b(p + 1, 0, q // 2, q % 2)
                    for si in range(2):
                        for h2_ in range(2):
                            emit_c(p + 1, 0, si, h2_)
                for s in range(SL):
                    for j in range(11):
                        u = HC * p + 11 * s + j
                        if wd_info[u][0]:
                            continue
                        if u >= 3:
                            sync.wait_ge(sem["FR_D"], u - 2)
                        sync.dma_start(out=WDB[u % 3][:],
                                       in_=sd_img[s * 11 + j]).then_inc(LWD[u % 3], 16)
                if p + 1 < n_pass:
                    for q in range(3):
                        emit_gu(p + 1, q // 2, q % 2)
            for b in range(NXB):
                n_st = len([1 for gi in range(4 * n_pass) if gi % NXB == b])
                if n_st:
                    sync.wait_ge(STX[b], 16 * n_st)

        @block.gpsimd
        def _(gpsimd):
            for u in range(NUNITS):
                kind, i = unit_kind(u)
                gpsimd.wait_ge(sem["P_DQ"], u + 1)
                if kind == "B":
                    dst, w = sb_img[i], D
                elif kind == "C":
                    dst, w = sc_img[i], NCH * 512
                elif kind == "G":
                    dst, w = sg_img[i], D
                elif kind == "U":
                    dst, w = su_img[i], D
                else:
                    dst, w = sd_img[i], D
                gpsimd.dma_start(out=dst, in_=WOUT[u % 8][:, :w]).then_inc(PWR[u % 8], 16)

            for p in range(n_pass):
                for s in range(SL):
                    for j in range(11):
                        u = HC * p + 11 * s + j
                        if not wd_info[u][0]:
                            continue
                        gpsimd.wait_ge(sem["FR_D"], u - 2)
                        pu = u - 3
                        pp, ps = pu // 44, (pu % 44) // 11
                        gpsimd.wait_ge(sem["MM_DN"], 4 * pp + ps + 1)
                        gpsimd.dma_start(out=WDB[u % 3][:],
                                         in_=sd_img[s * 11 + j]).then_inc(LWDS[u % 3], 16)

        @block.scalar
        def _(scalar):
            for i in range(8):
                scalar.wait_ge(PWR[i], PWR_FINAL[i])
            scalar.wait_ge(sem["LDC"], 64)

            def emit_xload(pp, g):
                gi = 4 * pp + g
                b = gi % NXB
                if gi >= NXB:
                    scalar.wait_ge(STX[b], 16 * (gi // NXB))
                scalar.dma_start(out=XT[b][:],
                                 in_=xin[gi * 128:(gi + 1) * 128, :]).then_inc(LDX[b], 16)

            def act_ln1(pp, groups):
                for g in groups:
                    k = 8 * pp + g
                    scalar.wait_ge(LDX[xb(pp, g)], ldx_val(pp, g))
                    if k >= 1:
                        scalar.wait_ge(sem["SQ_A"], k)
                    scalar.activation(TRA[:], XT[xb(pp, g)][:], AF.Square,
                                      accum_out=S2T[:, g:g + 1]).then_inc(sem["SQ_A"], 1)
                for g in groups:
                    k = 8 * pp + g
                    scalar.wait_ge(sem["FIN"], k + 1)
                    scalar.activation(STDT[:, g:g + 1], VART[:, g:g + 1],
                                      AF.Sqrt).then_inc(sem["SQRT_A"], 1)
                    scalar.wait_ge(sem["APPLY"], k + 1)
                    if pp > 0:
                        scalar.wait_ge(sem["MM_AT"], 16 * pp)
                    scalar.dma_start(out=HFM1[:, :, g * 128:(g + 1) * 128],
                                     in_=HTM[k % 2][:], transpose=True).then_inc(TPS[k % 4], 16)

            for g in range(G):
                emit_xload(0, g)
            act_ln1(0, (0, 1, 2, 3))
            for p in range(n_pass):
                if p + 1 < n_pass:
                    emit_xload(p + 1, 0)
                    emit_xload(p + 1, 1)
                for n in range(NCH):
                    if n == 0 and p > 0:
                        scalar.wait_ge(sem["MM_AT"], 16 * p)
                    scalar.wait_ge(sem["MM_ST"], 16 * p + 8 + n + 1)
                    scalar.activation(SFM[:, n, :], PS[:, n, :],
                                      AF.Tanh).then_inc(sem["TANH"], 1)
                for g in range(G):
                    k = 8 * p + 4 + g
                    scalar.wait_ge(sem["EV_AT"], 16 * p + 2 * g + 10)
                    scalar.wait_ge(sem["SQ_A"], k)
                    scalar.activation(TRA[:], XT[xb(p, g)][:], AF.Square,
                                      accum_out=S2T[:, 4 + g:5 + g]).then_inc(sem["SQ_A"], 1)
                    scalar.wait_ge(sem["FIN"], k + 1)
                    scalar.activation(STDT[:, 4 + g:5 + g], VART[:, 4 + g:5 + g],
                                      AF.Sqrt).then_inc(sem["SQRT_A"], 1)
                    scalar.wait_ge(sem["APPLY"], k + 1)
                    if p > 0:
                        scalar.wait_ge(sem["MM_U"], HC * p)
                    scalar.dma_start(out=HFM2[:, :, g * 128:(g + 1) * 128],
                                     in_=HTM[k % 2][:], transpose=True).then_inc(TPS[k % 4], 16)
                for hc in range(HC):
                    if hc == 0 and p > 0:
                        scalar.wait_ge(sem["MM_DN"], 4 * p)
                    scalar.wait_ge(sem["MM_G"], HC * p + hc + 1)
                    scalar.activation(GFM[:, hc, :], PS[:, 4 + hc % 2, :],
                                      AF.Silu, scale=SGC[:, hc:hc + 1]).then_inc(sem["SILU"], 1)
                if p + 1 < n_pass:
                    act_ln1(p + 1, (0, 1))
                for g in (0, 1):
                    scalar.wait_ge(sem["EV_DN"], 16 * p + 13 + g)
                    scalar.dma_start(out=out[(4 * p + g) * 128:(4 * p + g + 1) * 128, :],
                                     in_=XT[xb(p, g)][:]).then_inc(STX[xb(p, g)], 16)
                if p + 1 < n_pass:
                    emit_xload(p + 1, 2)
                    emit_xload(p + 1, 3)
                    act_ln1(p + 1, (2, 3))
                for g in (2, 3):
                    scalar.wait_ge(sem["EV_DN"], 16 * p + 13 + g)
                    scalar.dma_start(out=out[(4 * p + g) * 128:(4 * p + g + 1) * 128, :],
                                     in_=XT[xb(p, g)][:]).then_inc(STX[xb(p, g)], 16)

        @block.vector
        def _(vector):
            for u in range(NUNITS):
                kind, i = unit_kind(u)
                if u >= 8:
                    vector.wait_ge(PWR[u % 8], 16 * (u // 8))
                if kind in ("B", "C"):
                    w = D if kind == "B" else NCH * 512
                    vector.wait_ge(PLD1[u % 2], 16 * (u // 2 + 1))
                    vector.tensor_copy(WOUT[u % 8][:, :w],
                                       BF32[u % 2][:, :w]).then_inc(sem["P_DQ"], 1)
                elif kind in ("G", "U"):
                    o = u - (NU_B + NU_C)
                    vector.wait_ge(PLD2[o % 10], 16 * (o // 10 + 1))
                    vector.tensor_scalar_add(WOUT[u % 8][:, :D],
                                             LQ[o % 10][:], -128.0).then_inc(sem["P_DQ"], 1)
                else:
                    o = u - (NU_B + NU_C)
                    vector.wait_ge(PLD2[o % 10], 16 * (o // 10 + 1))
                    vector.wait_ge(sem["LSDP"], 16 * (i // 11 + 1))
                    vector.scalar_tensor_tensor(WOUT[u % 8][:, :D], LQ[o % 10][:],
                                                -128.0, SDP[:], ALU.add,
                                                ALU.mult).then_inc(sem["P_DQ"], 1)

            for i in range(8):
                vector.wait_ge(PWR[i], PWR_FINAL[i])
            vector.wait_ge(sem["LDC"], 64)
            ins = None
            for c in range(DC):
                ins = vector.tensor_scalar_mul(DDIAG[:, c, :], IDENT[:], DCOL[:, c:c + 1])
            ins.then_inc(sem["DDIAG_OK"], 1)

            dsc = {"n": 0}

            def ds_inc(ins):
                dsc["n"] += 1
                ins.then_inc(sem["DS"], 1)

            def ds_wait():
                vector.wait_ge(sem["DS"], dsc["n"])

            def ln_finalize(k, i):
                vector.wait_ge(sem["SQ_A"], k + 1)
                ds_wait()
                ds_inc(vector.tensor_scalar_mul(MEANT[:, i:i + 1], S1T[:, i:i + 1], 1.0 / D))
                ds_wait()
                ds_inc(vector.tensor_scalar(MSQT[:, i:i + 1], MEANT[:, i:i + 1],
                                            MEANT[:, i:i + 1], -EPS, ALU.mult, ALU.add))
                ds_wait()
                vector.scalar_tensor_tensor(VART[:, i:i + 1], S2T[:, i:i + 1], 1.0 / D,
                                            MSQT[:, i:i + 1], ALU.mult,
                                            ALU.subtract).then_inc(sem["FIN"], 1)
                vector.wait_ge(sem["SQRT_A"], k + 1)
                ds_inc(vector.reciprocal(RSTDT[:, i:i + 1], STDT[:, i:i + 1]))

            def htm_free(k):
                if k >= 2:
                    vector.wait_ge(TPS[(k - 2) % 4], 16 * ((k - 2) // 4 + 1))

            def ln_apply(k, i, b):
                ds_wait()
                vector.tensor_scalar(HTM[k % 2][:], XT[b][:], MEANT[:, i:i + 1],
                                     RSTDT[:, i:i + 1], ALU.subtract,
                                     ALU.mult).then_inc(sem["APPLY"], 1)

            def dve_ln1(pp, groups):
                for g in groups:
                    k = 8 * pp + g
                    vector.wait_ge(LDX[xb(pp, g)], ldx_val(pp, g))
                    htm_free(k)
                    ds_inc(vector.tensor_scalar(HTM[k % 2][:], XT[xb(pp, g)][:], 1.0, 0.0,
                                                ALU.mult, ALU.add,
                                                accum_out=S1T[:, g:g + 1]))
                    ln_finalize(k, g)
                    ln_apply(k, g, xb(pp, g))

            dve_ln1(0, (0, 1, 2, 3))
            for p in range(n_pass):
                def at_evac(m):
                    g, s = AT_TILES[m]
                    vector.wait_ge(sem["MM_AT"], 16 * p + m + 1)
                    vector.scalar_tensor_tensor(
                        XT[xb(p, g)][:, s * 512:(s + 1) * 512], PS[:, AT_BANK[m], :],
                        1.0, XT[xb(p, g)][:, s * 512:(s + 1) * 512], ALU.mult, ALU.add,
                        accum_out=S1PT[:, g, s:s + 1]).then_inc(sem["EV_AT"], 1)

                for m in range(8):
                    at_evac(m)
                for g in range(G):
                    at_evac(8 + 2 * g)
                    at_evac(8 + 2 * g + 1)
                    k = 8 * p + 4 + g
                    vector.wait_ge(sem["EV_AT"], 16 * p + 2 * g + 10)
                    htm_free(k)
                    ds_inc(vector.tensor_scalar(TR4[:], S1PT[:, g, :], 1.0, 0.0, ALU.mult,
                                                ALU.add, accum_out=S1T[:, 4 + g:5 + g]))
                    ln_finalize(k, 4 + g)
                    ln_apply(k, 4 + g, xb(p, g))
                for hc in range(HC):
                    vector.wait_ge(sem["SILU"], HC * p + hc + 1)
                    vector.wait_ge(sem["MM_U"], HC * p + hc + 1)
                    vector.scalar_tensor_tensor(GFM[:, hc, :], PS[:, 6 + hc % 2, :],
                                                SUC[:, hc:hc + 1], GFM[:, hc, :],
                                                ALU.mult, ALU.mult).then_inc(sem["GMUL"], 1)
                if p + 1 < n_pass:
                    dve_ln1(p + 1, (0, 1))
                for s in range(SL):
                    for g in range(G):
                        vector.wait_ge(sem["MM_DN"], 4 * p + s + 1)
                        vector.tensor_tensor(XT[xb(p, g)][:, s * 512:(s + 1) * 512],
                                             XT[xb(p, g)][:, s * 512:(s + 1) * 512],
                                             PS[:, (s % 2) * 4 + g, :],
                                             ALU.add).then_inc(sem["EV_DN"], 1)
                if p + 1 < n_pass:
                    dve_ln1(p + 1, (2, 3))

        @block.tensor
        def _(tensor):
            tensor.wait_ge(sem["DDIAG_OK"], 1)
            for p in range(n_pass):
                for half in range(2):
                    lo, hi = half * 256, half * 256 + 256
                    for n in range(NCH):
                        if n == 0:
                            for i in ((0, 1) if half == 0 else (2, 3)):
                                tensor.wait_ge(TPS[i], 16 * (2 * p + 1))
                        u = 16 * p + 8 * half + n
                        if half == 0 and p > 0:
                            tensor.wait_ge(sem["EV_DN"], 16 * (p - 1) + 9 + n)
                        ins = None
                        for c in range(DC):
                            if c % 8 == 0:
                                h = 2 * u + c // 8
                                tensor.wait_ge(LWB[h % 6], 16 * (h // 6 + 1))
                            ins = tensor.matmul(PS[:, n, lo:hi],
                                                WBB[(2 * u + c // 8) % 6][:, c % 8, :],
                                                HFM1[:, c, lo:hi],
                                                start=(c == 0), stop=(c == DC - 1))
                        ins.then_inc(sem["MM_ST"], 1)
                for m in range(16):
                    g, s = AT_TILES[m]
                    bank = AT_BANK[m]
                    if m == 0:
                        tensor.wait_ge(sem["TANH"], 8 * p + 8)
                    if 16 * p + m - 3 >= 1:
                        tensor.wait_ge(sem["EV_AT"], 16 * p + m - 3)
                    for sub in range(4):
                        dcg = s * 4 + sub
                        tensor.matmul(PS[:, bank, sub * 128:(sub + 1) * 128],
                                      HFM1[:, dcg, g * 128:(g + 1) * 128],
                                      DDIAG[:, dcg, :],
                                      start=(sub == 0), stop=False)
                    ins = None
                    for n in range(NCH):
                        if n % 4 == 0:
                            tensor.wait_ge(LWC[2 * (s % 2) + n // 4],
                                           16 * (2 * p + s // 2 + 1))
                        ins = tensor.matmul(PS[:, bank, :],
                                            SFM[:, n, g * 128:(g + 1) * 128],
                                            WCB[2 * (s % 2) + n // 4][:, n % 4, :],
                                            start=False, stop=(n == NCH - 1))
                    ins.then_inc(sem["MM_AT"], 1)

                def gate_chain(hc, lo, hi, inc):
                    t = 2 * (HC * p + hc)
                    ins = None
                    for c in range(DC):
                        ins = tensor.matmul(PS[:, 4 + hc % 2, lo:hi], WGU[t % 3][:, c, :],
                                            HFM2[:, c, lo:hi],
                                            start=(c == 0), stop=(c == DC - 1))
                    if inc:
                        ins.then_inc(sem["MM_G"], 1)

                def up_chain(hc, lo, hi, inc):
                    t = 2 * (HC * p + hc) + 1
                    ins = None
                    for c in range(DC):
                        ins = tensor.matmul(PS[:, 6 + hc % 2, lo:hi], WGU[t % 3][:, c, :],
                                            HFM2[:, c, lo:hi],
                                            start=(c == 0), stop=(c == DC - 1))
                    if inc:
                        ins.then_inc(sem["MM_U"], 1)

                for hc in (0, 1):
                    if hc == 0:
                        tensor.wait_ge(TPS[0], 16 * (2 * p + 2))
                        tensor.wait_ge(TPS[1], 16 * (2 * p + 2))
                    t = 2 * (HC * p + hc)
                    tensor.wait_ge(LWGU[t % 3], 16 * (t // 3 + 1))
                    if p > 0:
                        tensor.wait_ge(sem["EV_DN"], 16 * (p - 1) + 13 + hc)
                    gate_chain(hc, 0, 256, False)
                for hc in (0, 1):
                    if hc == 0:
                        tensor.wait_ge(TPS[2], 16 * (2 * p + 2))
                        tensor.wait_ge(TPS[3], 16 * (2 * p + 2))
                    gate_chain(hc, 256, 512, True)
                for hc in (0, 1):
                    t = 2 * (HC * p + hc) + 1
                    tensor.wait_ge(LWGU[t % 3], 16 * (t // 3 + 1))
                    if p > 0:
                        tensor.wait_ge(sem["EV_DN"], 16 * (p - 1) + 15 + hc)
                    up_chain(hc, 0, 512, True)
                for hc in range(2, HC):
                    t = 2 * (HC * p + hc)
                    tensor.wait_ge(LWGU[t % 3], 16 * (t // 3 + 1))
                    tensor.wait_ge(sem["SILU"], HC * p + hc - 1)
                    gate_chain(hc, 0, 512, True)
                    tensor.wait_ge(LWGU[(t + 1) % 3], 16 * ((t + 1) // 3 + 1))
                    tensor.wait_ge(sem["GMUL"], HC * p + hc - 1)
                    up_chain(hc, 0, 512, True)

                for s in range(SL):
                    for j in range(11):
                        u = HC * p + 11 * s + j
                        spec, slot, val = wd_info[u]
                        tensor.wait_ge(LWDS[slot] if spec else LWD[slot], val)
                        tensor.wait_ge(sem["GMUL"], HC * p + 4 * j + 4)
                        for q in range(4):
                            hc = 4 * j + q
                            for g in range(G):
                                bank = (s % 2) * 4 + g
                                if hc == 0:
                                    if s >= 2:
                                        tensor.wait_ge(sem["EV_DN"], 16 * p + 4 * (s - 2) + g + 1)
                                    elif s == 0:
                                        tensor.wait_ge(sem["EV_AT"],
                                                       16 * p + 13 + (2, 3, 0, 1)[g])
                                    else:
                                        if g < 2:
                                            tensor.wait_ge(sem["SILU"], HC * p + 43 + g)
                                        else:
                                            tensor.wait_ge(sem["GMUL"], HC * p + 41 + g)
                                ins = tensor.matmul(PS[:, bank, :],
                                                    GFM[:, hc, g * 128:(g + 1) * 128],
                                                    WDB[u % 3][:, q * 512:(q + 1) * 512],
                                                    start=(hc == 0), stop=(hc == HC - 1))
                                if hc == HC - 1 and g == G - 1:
                                    ins.then_inc(sem["MM_DN"], 1)
                                elif hc == HC - 1 and g == G - 2:
                                    ins.then_inc(sem["FR_D"], 1)
                        if j < 10:
                            ins.then_inc(sem["FR_D"], 1)

        return nc


def kernel(L0_attn_x, L0_attn_B, L0_attn_C, L0_attn_D,
           L0_ffn_wq_gate, L0_ffn_s_gate, L0_ffn_wq_up, L0_ffn_s_up,
           L0_ffn_wq_down, L0_ffn_s_down):
    x = np.asarray(L0_attn_x, dtype=np.float32).reshape(NCORE * TOK, D)
    B = np.ascontiguousarray(np.asarray(L0_attn_B, dtype=np.float32))
    C = np.ascontiguousarray(np.asarray(L0_attn_C, dtype=np.float32))
    Dv = np.asarray(L0_attn_D, dtype=np.float32)
    wg = np.ascontiguousarray(np.asarray(L0_ffn_wq_gate, dtype=np.int32))
    wu = np.ascontiguousarray(np.asarray(L0_ffn_wq_up, dtype=np.int32))
    wd = np.ascontiguousarray(np.asarray(L0_ffn_wq_down, dtype=np.int32))
    sg = np.asarray(L0_ffn_s_gate, dtype=np.float32)
    su = np.asarray(L0_ffn_s_up, dtype=np.float32)
    sd = np.asarray(L0_ffn_s_down, dtype=np.float32)

    sgc = np.ascontiguousarray(sg.reshape(HC, 128).T)
    suc = np.ascontiguousarray(su.reshape(HC, 128).T)
    sdq = np.ascontiguousarray(
        np.broadcast_to(sd.reshape(SL, 1, 1, 512), (SL, 128, 4, 512)).reshape(SL, 128, D))
    dcol = np.ascontiguousarray(Dv.reshape(DC, 128).T)
    ident = np.eye(128, dtype=np.float32).astype(ml_dtypes.bfloat16)

    wgi = np.ascontiguousarray(
        wg.reshape(HC, 128, DC, 128).transpose(0, 3, 2, 1).reshape(HC, 128, DC * 128))
    wui = np.ascontiguousarray(
        wu.reshape(HC, 128, DC, 128).transpose(0, 3, 2, 1).reshape(HC, 128, DC * 128))
    wdi = np.ascontiguousarray(
        wd.reshape(SL, 512, 11, 4, 128).transpose(0, 2, 4, 3, 1).reshape(SL * 11, 128, D))
    bmi = np.ascontiguousarray(
        B.reshape(NCH, 128, DC, 128).transpose(0, 3, 2, 1).reshape(NCH, 128, DC * 128))
    cmi = np.ascontiguousarray(
        C.reshape(SL, 512, NCH, 128).transpose(0, 3, 2, 1).reshape(SL, 128, NCH * 512))

    nc = build_nc(TOK // T)
    common = dict(wgi=wgi, wui=wui, wdi=wdi, bmi=bmi, cmi=cmi, sgc=sgc, suc=suc,
                  sdq=sdq, dcol=dcol, ident=ident)
    in_maps = [dict(xin=np.ascontiguousarray(x[i * TOK:(i + 1) * TOK]), **common)
               for i in range(NCORE)]
    res = run_bass_kernel_spmd(nc, in_maps, core_ids=list(range(NCORE)))
    outs = [res.results[i]["out"] for i in range(NCORE)]
    return np.concatenate(outs, axis=0).reshape(16, 2048, 2048).astype(np.float32)


# version 20
# speedup vs baseline: 1.0875x; 1.0790x over previous
import numpy as np
import ml_dtypes
import concourse.bass as bass
import concourse.mybir as mybir
from concourse.bass_utils import run_bass_kernel_spmd
from contextlib import ExitStack

F32 = mybir.dt.float32
BF16 = mybir.dt.bfloat16
I32 = mybir.dt.int32
AF = mybir.ActivationFunctionType
ALU = mybir.AluOpType

NCORE = 8
TOK = 4096
D = 2048
NST = 1024
HID = 5632
T = 512
G = 4
DC = D // 128
NCH = NST // 128
HC = HID // 128
SL = 4
NXB = 7
EPS = 1e-5

NU_B, NU_C, NU_G, NU_U, NU_D = NCH, SL, HC, HC, HC
NUNITS = NU_B + NU_C + NU_G + NU_U + NU_D


def unit_kind(u):
    if u < NU_B:
        return ("B", u)
    u -= NU_B
    if u < NU_C:
        return ("C", u)
    u -= NU_C
    if u < NU_G:
        return ("G", u)
    u -= NU_G
    if u < NU_U:
        return ("U", u)
    return ("D", u - NU_U)


def build_nc(n_pass):
    nc = bass.Bass("TRN2", target_bir_lowering=False)

    xin = nc.dram_tensor("xin", [TOK, D], F32, kind="ExternalInput").ap()
    wgi = nc.dram_tensor("wgi", [HC, 128, DC * 128], I32, kind="ExternalInput").ap()
    wui = nc.dram_tensor("wui", [HC, 128, DC * 128], I32, kind="ExternalInput").ap()
    wdi = nc.dram_tensor("wdi", [SL * 11, 128, D], I32, kind="ExternalInput").ap()
    bmi = nc.dram_tensor("bmi", [NCH, 128, DC * 128], F32, kind="ExternalInput").ap()
    cmi = nc.dram_tensor("cmi", [SL, 128, NCH * 512], F32, kind="ExternalInput").ap()
    sgc = nc.dram_tensor("sgc", [128, HC], F32, kind="ExternalInput").ap()
    suc = nc.dram_tensor("suc", [128, HC], F32, kind="ExternalInput").ap()
    sdq = nc.dram_tensor("sdq", [SL, 128, D], F32, kind="ExternalInput").ap()
    dcol = nc.dram_tensor("dcol", [128, DC], F32, kind="ExternalInput").ap()
    ident = nc.dram_tensor("ident", [128, 128], BF16, kind="ExternalInput").ap()
    out = nc.dram_tensor("out", [TOK, D], F32, kind="ExternalOutput").ap()

    sg_img = nc.dram_tensor("sg_img", [HC, 128, DC, 128], BF16, kind="Internal").ap()
    su_img = nc.dram_tensor("su_img", [HC, 128, DC, 128], BF16, kind="Internal").ap()
    sb_img = nc.dram_tensor("sb_img", [NCH, 128, DC, 128], BF16, kind="Internal").ap()
    sc_img = nc.dram_tensor("sc_img", [SL, 128, NCH, 512], BF16, kind="Internal").ap()
    sd_img = nc.dram_tensor("sd_img", [SL * 11, 128, D], BF16, kind="Internal").ap()

    with ExitStack() as ctx:
        E = ctx.enter_context

        SGC = E(nc.sbuf_tensor("SGC", [128, HC], F32))
        SUC = E(nc.sbuf_tensor("SUC", [128, HC], F32))
        DCOL = E(nc.sbuf_tensor("DCOL", [128, DC], F32))
        IDENT = E(nc.sbuf_tensor("IDENT", [128, 128], BF16))
        DDIAG = E(nc.sbuf_tensor("DDIAG", [128, DC, 128], BF16))

        S1T = E(nc.sbuf_tensor("S1T", [128, 8], F32))
        S2T = E(nc.sbuf_tensor("S2T", [128, 8], F32))
        MEANT = E(nc.sbuf_tensor("MEANT", [128, 8], F32))
        MSQT = E(nc.sbuf_tensor("MSQT", [128, 8], F32))
        VART = E(nc.sbuf_tensor("VART", [128, 8], F32))
        STDT = E(nc.sbuf_tensor("STDT", [128, 8], F32))
        RSTDT = E(nc.sbuf_tensor("RSTDT", [128, 8], F32))
        S1PT = E(nc.sbuf_tensor("S1PT", [128, G, SL], F32))
        TR4 = E(nc.sbuf_tensor("TR4", [128, SL], F32))

        with ExitStack() as pctx:
            P_ = pctx.enter_context
            LQ = [P_(nc.sbuf_tensor(f"LQ{i}", [128, D], I32)) for i in range(10)]
            BF32 = [P_(nc.sbuf_tensor(f"BF32{i}", [128, 2 * D], F32)) for i in range(2)]
            WOUT = [P_(nc.sbuf_tensor(f"WOUT{i}", [128, 2 * D], BF16)) for i in range(8)]
            SDP = P_(nc.sbuf_tensor("SDP", [128, D], F32))

        XT = [E(nc.sbuf_tensor(f"XT{b}", [128, D], F32)) for b in range(NXB)]
        HTM = [E(nc.sbuf_tensor(f"HTM{i}", [128, D], BF16)) for i in range(2)]
        HFM1 = E(nc.sbuf_tensor("HFM1", [128, DC, T], BF16))
        HFM2 = E(nc.sbuf_tensor("HFM2", [128, DC, T], BF16))
        SFM = E(nc.sbuf_tensor("SFM", [128, NCH, T], BF16))
        GFM = E(nc.sbuf_tensor("GFM", [128, HC, T], BF16))
        WGU = [E(nc.sbuf_tensor(f"WGU{i}", [128, DC, 128], BF16)) for i in range(3)]
        WBB = [E(nc.sbuf_tensor(f"WBB{i}", [128, DC, 128], BF16)) for i in range(3)]
        WCB = [E(nc.sbuf_tensor(f"WCB{i}", [128, NCH // 2, T], BF16)) for i in range(4)]
        WDB = [E(nc.sbuf_tensor(f"WDB{i}", [128, D], BF16)) for i in range(3)]

        PS = E(nc.psum_tensor("PS", [128, 8, 512], F32))

        sem = {}
        for name in ["LDC", "LSDP", "P_DQ",
                     "MM_ST", "MM_AT", "MM_G", "MM_U", "MM_DN", "FR_D",
                     "EV_AT", "EV_DN", "TANH", "SILU", "GMUL",
                     "APPLY", "SQ_A", "FIN", "SQRT_A", "DDIAG_OK", "DS"]:
            sem[name] = E(nc.semaphore(name))
        PLD1 = [E(nc.semaphore(f"PLD1_{i}")) for i in range(4)]
        PLD2 = [E(nc.semaphore(f"PLD2_{i}")) for i in range(10)]
        PWR = [E(nc.semaphore(f"PWR{i}")) for i in range(8)]
        LDX = [E(nc.semaphore(f"LDX{i}")) for i in range(NXB)]
        STX = [E(nc.semaphore(f"STX{i}")) for i in range(NXB)]
        LWB = [E(nc.semaphore(f"LWB{i}")) for i in range(3)]
        LWC = [E(nc.semaphore(f"LWC{i}")) for i in range(4)]
        LWGU = [E(nc.semaphore(f"LWGU{i}")) for i in range(3)]
        LWD = [E(nc.semaphore(f"LWD{i}")) for i in range(3)]
        LWDS = [E(nc.semaphore(f"LWDS{i}")) for i in range(3)]
        TPS = [E(nc.semaphore(f"TPS{i}")) for i in range(4)]

        block = E(nc.Block())
        PWR_FINAL = [16 * len([1 for u in range(NUNITS) if u % 8 == i])
                     for i in range(8)]

        wd_info = {}
        _cnt_wd = {}
        for _p in range(n_pass):
            for _s in range(SL):
                for _j in range(11):
                    _u = HC * _p + 11 * _s + _j
                    special = _u >= 3 and (_u - 3) % 11 == 10
                    key = ("S" if special else "H", _u % 3)
                    _cnt_wd[key] = _cnt_wd.get(key, 0) + 16
                    wd_info[_u] = (special, _u % 3, _cnt_wd[key])

        AT_TILES = [(g, 2 * sp + si) for sp in range(2) for g in range(G)
                    for si in range(2)]
        AT_BANK = [2 * (s % 2) + (g % 2) for (g, s) in AT_TILES]

        ST_ORDER = []
        _a = _b = 0
        while _a < NCH or _b < NCH:
            if _a < NCH and (_a - _b < 2 or _b >= NCH):
                ST_ORDER.append((_a, 0))
                _a += 1
            else:
                ST_ORDER.append((_b, 1))
                _b += 1
        ST_POS = {nh: c for c, nh in enumerate(ST_ORDER)}

        def xb(p, g):
            return (4 * p + g) % NXB

        def ldx_val(p, g):
            return 16 * ((4 * p + g) // NXB + 1)

        @block.sync
        def _(sync):
            for apsrc, sb in [(sgc, SGC), (suc, SUC),
                              (dcol, DCOL), (ident, IDENT)]:
                sync.dma_start(out=sb[:], in_=apsrc).then_inc(sem["LDC"], 16)

            def emit_prep_load(u):
                kind, i = unit_kind(u)
                if kind == "D" and i % 11 == 0:
                    s_ = i // 11
                    if s_ > 0:
                        sync.wait_ge(sem["P_DQ"], NU_B + NU_C + NU_G + NU_U + 11 * s_)
                    sync.dma_start(out=SDP[:], in_=sdq[s_]).then_inc(sem["LSDP"], 16)
                if kind in ("B", "C"):
                    if u >= 2:
                        sync.wait_ge(sem["P_DQ"], u - 1)
                    w = D if kind == "B" else NCH * 512
                    src = bmi[i] if kind == "B" else cmi[i]
                    sync.dma_start(out=BF32[u % 2][:, :w], in_=src).then_inc(PLD1[u % 2], 16)
                else:
                    o = u - (NU_B + NU_C)
                    if o >= 10:
                        sync.wait_ge(sem["P_DQ"], u - 10 + 1)
                    src = wgi[i] if kind == "G" else (wui[i] if kind == "U" else wdi[i])
                    sync.dma_start(out=LQ[o % 10][:], in_=src).then_inc(PLD2[o % 10], 16)

            for u in range(NUNITS):
                emit_prep_load(u)

            for i in range(8):
                sync.wait_ge(PWR[i], PWR_FINAL[i])

            def emit_b(p, half, n):
                u = 16 * p + 8 * half + n
                if u >= 3:
                    sync.wait_ge(sem["MM_ST"], u - 2)
                sync.dma_start(out=WBB[u % 3][:], in_=sb_img[n]).then_inc(LWB[u % 3], 16)

            def emit_c(p, sp, si, h2_):
                s = 2 * sp + si
                slot = 2 * si + h2_
                if p > 0 or sp > 0:
                    pm = (16 * p + 8 * (sp - 1) + 7 + si) if sp > 0 \
                        else (16 * (p - 1) + 15 + si)
                    sync.wait_ge(sem["MM_AT"], pm)
                sync.dma_start(out=WCB[slot][:],
                               in_=sc_img[s][:, h2_ * 4:(h2_ + 1) * 4, :]
                               ).then_inc(LWC[slot], 16)

            def emit_gu(p, hc, is_up):
                t = 2 * (HC * p + hc) + is_up
                if t >= 3:
                    hc3 = (t - 3) // 2
                    if (t - 3) % 2 == 0:
                        sync.wait_ge(sem["MM_G"], hc3 + 1)
                    else:
                        sync.wait_ge(sem["MM_U"], hc3 + 1)
                sync.dma_start(out=WGU[t % 3][:],
                               in_=(su_img if is_up else sg_img)[hc]
                               ).then_inc(LWGU[t % 3], 16)

            for p in range(n_pass):
                for half in range(2):
                    for n in range(NCH):
                        if p > 0 and half == 0 and n < 3:
                            continue
                        emit_b(p, half, n)
                for sp in range(2):
                    for si in range(2):
                        for h2_ in range(2):
                            if p > 0 and sp == 0:
                                continue
                            emit_c(p, sp, si, h2_)
                for hc in range(HC):
                    for is_up in range(2):
                        if p > 0 and 2 * hc + is_up < 3:
                            continue
                        emit_gu(p, hc, is_up)
                if p + 1 < n_pass:
                    for n in range(3):
                        emit_b(p + 1, 0, n)
                    for si in range(2):
                        for h2_ in range(2):
                            emit_c(p + 1, 0, si, h2_)
                for s in range(SL):
                    for j in range(11):
                        u = HC * p + 11 * s + j
                        if wd_info[u][0]:
                            continue
                        if u >= 3:
                            sync.wait_ge(sem["FR_D"], u - 2)
                        sync.dma_start(out=WDB[u % 3][:],
                                       in_=sd_img[s * 11 + j]).then_inc(LWD[u % 3], 16)
                if p + 1 < n_pass:
                    for q in range(3):
                        emit_gu(p + 1, q // 2, q % 2)
            for b in range(NXB):
                n_st = len([1 for gi in range(4 * n_pass) if gi % NXB == b])
                if n_st:
                    sync.wait_ge(STX[b], 16 * n_st)

        @block.gpsimd
        def _(gpsimd):
            for u in range(NUNITS):
                kind, i = unit_kind(u)
                gpsimd.wait_ge(sem["P_DQ"], u + 1)
                if kind == "B":
                    dst, w = sb_img[i], D
                elif kind == "C":
                    dst, w = sc_img[i], NCH * 512
                elif kind == "G":
                    dst, w = sg_img[i], D
                elif kind == "U":
                    dst, w = su_img[i], D
                else:
                    dst, w = sd_img[i], D
                gpsimd.dma_start(out=dst, in_=WOUT[u % 8][:, :w]).then_inc(PWR[u % 8], 16)

            for p in range(n_pass):
                for s in range(SL):
                    for j in range(11):
                        u = HC * p + 11 * s + j
                        if not wd_info[u][0]:
                            continue
                        gpsimd.wait_ge(sem["FR_D"], u - 2)
                        pu = u - 3
                        pp, ps = pu // 44, (pu % 44) // 11
                        gpsimd.wait_ge(sem["MM_DN"], 4 * pp + ps + 1)
                        gpsimd.dma_start(out=WDB[u % 3][:],
                                         in_=sd_img[s * 11 + j]).then_inc(LWDS[u % 3], 16)

        @block.scalar
        def _(scalar):
            for i in range(8):
                scalar.wait_ge(PWR[i], PWR_FINAL[i])
            scalar.wait_ge(sem["LDC"], 64)

            def emit_xload(pp, g):
                gi = 4 * pp + g
                b = gi % NXB
                if gi >= NXB:
                    scalar.wait_ge(STX[b], 16 * (gi // NXB))
                scalar.dma_start(out=XT[b][:],
                                 in_=xin[gi * 128:(gi + 1) * 128, :]).then_inc(LDX[b], 16)

            def act_ln1(pp, groups):
                for g in groups:
                    k = 8 * pp + g
                    scalar.wait_ge(sem["FIN"], k + 1)
                    scalar.activation(STDT[:, g:g + 1], VART[:, g:g + 1],
                                      AF.Sqrt).then_inc(sem["SQRT_A"], 1)
                    scalar.wait_ge(sem["APPLY"], k + 1)
                    if pp > 0:
                        scalar.wait_ge(sem["MM_AT"], 16 * pp)
                    scalar.dma_start(out=HFM1[:, :, g * 128:(g + 1) * 128],
                                     in_=HTM[k % 2][:], transpose=True).then_inc(TPS[k % 4], 16)

            for g in range(G):
                emit_xload(0, g)
            act_ln1(0, (0, 1, 2, 3))
            for p in range(n_pass):
                for n in range(NCH):
                    if n == 0 and p > 0:
                        scalar.wait_ge(sem["MM_AT"], 16 * p)
                    scalar.wait_ge(sem["MM_ST"], 16 * p + 8 + n + 1)
                    scalar.activation(SFM[:, n, :], PS[:, n, :],
                                      AF.Tanh).then_inc(sem["TANH"], 1)
                for g in range(G):
                    k = 8 * p + 4 + g
                    scalar.wait_ge(sem["EV_AT"], 16 * p + 2 * g + 10)
                    scalar.wait_ge(TPS[(k - 2) % 4], 16 * ((k - 2) // 4 + 1))
                    scalar.activation(HTM[k % 2][:], XT[xb(p, g)][:], AF.Square,
                                      accum_out=S2T[:, 4 + g:5 + g]).then_inc(sem["SQ_A"], 1)
                    scalar.wait_ge(sem["FIN"], k + 1)
                    scalar.activation(STDT[:, 4 + g:5 + g], VART[:, 4 + g:5 + g],
                                      AF.Sqrt).then_inc(sem["SQRT_A"], 1)
                    scalar.wait_ge(sem["APPLY"], k + 1)
                    if p > 0:
                        scalar.wait_ge(sem["MM_U"], HC * p)
                    scalar.dma_start(out=HFM2[:, :, g * 128:(g + 1) * 128],
                                     in_=HTM[k % 2][:], transpose=True).then_inc(TPS[k % 4], 16)
                if p + 1 < n_pass:
                    for g in range(3):
                        emit_xload(p + 1, g)
                for hc in range(HC):
                    if hc == 0 and p > 0:
                        scalar.wait_ge(sem["MM_DN"], 4 * p)
                    scalar.wait_ge(sem["MM_G"], HC * p + hc + 1)
                    scalar.activation(GFM[:, hc, :], PS[:, 4 + hc % 2, :],
                                      AF.Silu, scale=SGC[:, hc:hc + 1]).then_inc(sem["SILU"], 1)
                if p + 1 < n_pass:
                    act_ln1(p + 1, (0, 1, 2))
                scalar.wait_ge(sem["EV_DN"], 16 * p + 13)
                scalar.dma_start(out=out[(4 * p) * 128:(4 * p + 1) * 128, :],
                                 in_=XT[xb(p, 0)][:]).then_inc(STX[xb(p, 0)], 16)
                if p + 1 < n_pass:
                    emit_xload(p + 1, 3)
                    act_ln1(p + 1, (3,))
                for g in (1, 2, 3):
                    scalar.wait_ge(sem["EV_DN"], 16 * p + 13 + g)
                    scalar.dma_start(out=out[(4 * p + g) * 128:(4 * p + g + 1) * 128, :],
                                     in_=XT[xb(p, g)][:]).then_inc(STX[xb(p, g)], 16)

        @block.vector
        def _(vector):
            for u in range(NUNITS):
                kind, i = unit_kind(u)
                if u >= 8:
                    vector.wait_ge(PWR[u % 8], 16 * (u // 8))
                if kind in ("B", "C"):
                    w = D if kind == "B" else NCH * 512
                    vector.wait_ge(PLD1[u % 2], 16 * (u // 2 + 1))
                    vector.tensor_copy(WOUT[u % 8][:, :w],
                                       BF32[u % 2][:, :w]).then_inc(sem["P_DQ"], 1)
                elif kind in ("G", "U"):
                    o = u - (NU_B + NU_C)
                    vector.wait_ge(PLD2[o % 10], 16 * (o // 10 + 1))
                    vector.tensor_scalar_add(WOUT[u % 8][:, :D],
                                             LQ[o % 10][:], -128.0).then_inc(sem["P_DQ"], 1)
                else:
                    o = u - (NU_B + NU_C)
                    vector.wait_ge(PLD2[o % 10], 16 * (o // 10 + 1))
                    vector.wait_ge(sem["LSDP"], 16 * (i // 11 + 1))
                    vector.scalar_tensor_tensor(WOUT[u % 8][:, :D], LQ[o % 10][:],
                                                -128.0, SDP[:], ALU.add,
                                                ALU.mult).then_inc(sem["P_DQ"], 1)

            for i in range(8):
                vector.wait_ge(PWR[i], PWR_FINAL[i])
            vector.wait_ge(sem["LDC"], 64)
            ins = None
            for c in range(DC):
                ins = vector.tensor_scalar_mul(DDIAG[:, c, :], IDENT[:], DCOL[:, c:c + 1])
            ins.then_inc(sem["DDIAG_OK"], 1)

            dsc = {"n": 0}

            def ds_inc(ins):
                dsc["n"] += 1
                ins.then_inc(sem["DS"], 1)

            def ds_wait():
                vector.wait_ge(sem["DS"], dsc["n"])

            def ln_finalize(k, i, sqa=None):
                if sqa is not None:
                    vector.wait_ge(sem["SQ_A"], sqa)
                ds_wait()
                ds_inc(vector.tensor_scalar_mul(MEANT[:, i:i + 1], S1T[:, i:i + 1], 1.0 / D))
                ds_wait()
                ds_inc(vector.tensor_scalar(MSQT[:, i:i + 1], MEANT[:, i:i + 1],
                                            MEANT[:, i:i + 1], -EPS, ALU.mult, ALU.add))
                ds_wait()
                vector.scalar_tensor_tensor(VART[:, i:i + 1], S2T[:, i:i + 1], 1.0 / D,
                                            MSQT[:, i:i + 1], ALU.mult,
                                            ALU.subtract).then_inc(sem["FIN"], 1)
                vector.wait_ge(sem["SQRT_A"], k + 1)
                ds_inc(vector.reciprocal(RSTDT[:, i:i + 1], STDT[:, i:i + 1]))

            def htm_free(k):
                if k >= 2:
                    vector.wait_ge(TPS[(k - 2) % 4], 16 * ((k - 2) // 4 + 1))

            def ln_apply(k, i, b):
                ds_wait()
                vector.tensor_scalar(HTM[k % 2][:], XT[b][:], MEANT[:, i:i + 1],
                                     RSTDT[:, i:i + 1], ALU.subtract,
                                     ALU.mult).then_inc(sem["APPLY"], 1)

            def dve_ln1(pp, groups):
                for g in groups:
                    k = 8 * pp + g
                    vector.wait_ge(LDX[xb(pp, g)], ldx_val(pp, g))
                    htm_free(k)
                    ds_inc(vector.tensor_scalar(HTM[k % 2][:], XT[xb(pp, g)][:], 1.0, 0.0,
                                                ALU.mult, ALU.add,
                                                accum_out=S1T[:, g:g + 1]))
                    ds_wait()
                    ds_inc(vector.scalar_tensor_tensor(HTM[k % 2][:], XT[xb(pp, g)][:],
                                                       1.0, XT[xb(pp, g)][:], ALU.mult,
                                                       ALU.mult,
                                                       accum_out=S2T[:, g:g + 1]))
                    ln_finalize(k, g)
                    ln_apply(k, g, xb(pp, g))

            dve_ln1(0, (0, 1, 2, 3))
            for p in range(n_pass):
                def at_evac(m):
                    g, s = AT_TILES[m]
                    vector.wait_ge(sem["MM_AT"], 16 * p + m + 1)
                    vector.scalar_tensor_tensor(
                        XT[xb(p, g)][:, s * 512:(s + 1) * 512], PS[:, AT_BANK[m], :],
                        1.0, XT[xb(p, g)][:, s * 512:(s + 1) * 512], ALU.mult, ALU.add,
                        accum_out=S1PT[:, g, s:s + 1]).then_inc(sem["EV_AT"], 1)

                for m in range(8):
                    at_evac(m)
                for g in range(G):
                    at_evac(8 + 2 * g)
                    at_evac(8 + 2 * g + 1)
                    k = 8 * p + 4 + g
                    vector.wait_ge(sem["EV_AT"], 16 * p + 2 * g + 10)
                    htm_free(k)
                    ds_inc(vector.tensor_scalar(TR4[:], S1PT[:, g, :], 1.0, 0.0, ALU.mult,
                                                ALU.add, accum_out=S1T[:, 4 + g:5 + g]))
                    ln_finalize(k, 4 + g, sqa=4 * p + g + 1)
                    ln_apply(k, 4 + g, xb(p, g))
                for hc in range(HC):
                    vector.wait_ge(sem["SILU"], HC * p + hc + 1)
                    vector.wait_ge(sem["MM_U"], HC * p + hc + 1)
                    vector.scalar_tensor_tensor(GFM[:, hc, :], PS[:, 6 + hc % 2, :],
                                                SUC[:, hc:hc + 1], GFM[:, hc, :],
                                                ALU.mult, ALU.mult).then_inc(sem["GMUL"], 1)
                if p + 1 < n_pass:
                    dve_ln1(p + 1, (0, 1, 2))
                for s in range(SL):
                    for g in range(G):
                        vector.wait_ge(sem["MM_DN"], 4 * p + s + 1)
                        vector.tensor_tensor(XT[xb(p, g)][:, s * 512:(s + 1) * 512],
                                             XT[xb(p, g)][:, s * 512:(s + 1) * 512],
                                             PS[:, (s % 2) * 4 + g, :],
                                             ALU.add).then_inc(sem["EV_DN"], 1)
                if p + 1 < n_pass:
                    dve_ln1(p + 1, (3,))

        @block.tensor
        def _(tensor):
            tensor.wait_ge(sem["DDIAG_OK"], 1)
            for p in range(n_pass):
                for half in range(2):
                    lo, hi = half * 256, half * 256 + 256
                    for n in range(NCH):
                        if n == 0:
                            for i in ((0, 1) if half == 0 else (2, 3)):
                                tensor.wait_ge(TPS[i], 16 * (2 * p + 1))
                        u = 16 * p + 8 * half + n
                        tensor.wait_ge(LWB[u % 3], 16 * (u // 3 + 1))
                        if half == 0 and p > 0:
                            tensor.wait_ge(sem["EV_DN"], 16 * (p - 1) + 9 + n)
                        ins = None
                        for c in range(DC):
                            ins = tensor.matmul(PS[:, n, lo:hi], WBB[u % 3][:, c, :],
                                                HFM1[:, c, lo:hi],
                                                start=(c == 0), stop=(c == DC - 1))
                        ins.then_inc(sem["MM_ST"], 1)
                for m in range(16):
                    g, s = AT_TILES[m]
                    bank = AT_BANK[m]
                    if m == 0:
                        tensor.wait_ge(sem["TANH"], 8 * p + 8)
                    if 16 * p + m - 3 >= 1:
                        tensor.wait_ge(sem["EV_AT"], 16 * p + m - 3)
                    for sub in range(4):
                        dcg = s * 4 + sub
                        tensor.matmul(PS[:, bank, sub * 128:(sub + 1) * 128],
                                      HFM1[:, dcg, g * 128:(g + 1) * 128],
                                      DDIAG[:, dcg, :],
                                      start=(sub == 0), stop=False)
                    ins = None
                    for n in range(NCH):
                        if n % 4 == 0:
                            tensor.wait_ge(LWC[2 * (s % 2) + n // 4],
                                           16 * (2 * p + s // 2 + 1))
                        ins = tensor.matmul(PS[:, bank, :],
                                            SFM[:, n, g * 128:(g + 1) * 128],
                                            WCB[2 * (s % 2) + n // 4][:, n % 4, :],
                                            start=False, stop=(n == NCH - 1))
                    ins.then_inc(sem["MM_AT"], 1)

                def gate_chain(hc, lo, hi, inc):
                    t = 2 * (HC * p + hc)
                    ins = None
                    for c in range(DC):
                        ins = tensor.matmul(PS[:, 4 + hc % 2, lo:hi], WGU[t % 3][:, c, :],
                                            HFM2[:, c, lo:hi],
                                            start=(c == 0), stop=(c == DC - 1))
                    if inc:
                        ins.then_inc(sem["MM_G"], 1)

                def up_chain(hc, lo, hi, inc):
                    t = 2 * (HC * p + hc) + 1
                    ins = None
                    for c in range(DC):
                        ins = tensor.matmul(PS[:, 6 + hc % 2, lo:hi], WGU[t % 3][:, c, :],
                                            HFM2[:, c, lo:hi],
                                            start=(c == 0), stop=(c == DC - 1))
                    if inc:
                        ins.then_inc(sem["MM_U"], 1)

                for hc in (0, 1):
                    if hc == 0:
                        tensor.wait_ge(TPS[0], 16 * (2 * p + 2))
                        tensor.wait_ge(TPS[1], 16 * (2 * p + 2))
                    t = 2 * (HC * p + hc)
                    tensor.wait_ge(LWGU[t % 3], 16 * (t // 3 + 1))
                    if p > 0:
                        tensor.wait_ge(sem["EV_DN"], 16 * (p - 1) + 13 + hc)
                    gate_chain(hc, 0, 256, False)
                for hc in (0, 1):
                    if hc == 0:
                        tensor.wait_ge(TPS[2], 16 * (2 * p + 2))
                        tensor.wait_ge(TPS[3], 16 * (2 * p + 2))
                    gate_chain(hc, 256, 512, True)
                for hc in (0, 1):
                    t = 2 * (HC * p + hc) + 1
                    tensor.wait_ge(LWGU[t % 3], 16 * (t // 3 + 1))
                    if p > 0:
                        tensor.wait_ge(sem["EV_DN"], 16 * (p - 1) + 15 + hc)
                    up_chain(hc, 0, 512, True)
                for hc in range(2, HC):
                    t = 2 * (HC * p + hc)
                    tensor.wait_ge(LWGU[t % 3], 16 * (t // 3 + 1))
                    tensor.wait_ge(sem["SILU"], HC * p + hc - 1)
                    gate_chain(hc, 0, 512, True)
                    tensor.wait_ge(LWGU[(t + 1) % 3], 16 * ((t + 1) // 3 + 1))
                    tensor.wait_ge(sem["GMUL"], HC * p + hc - 1)
                    up_chain(hc, 0, 512, True)

                for s in range(SL):
                    for j in range(11):
                        u = HC * p + 11 * s + j
                        spec, slot, val = wd_info[u]
                        tensor.wait_ge(LWDS[slot] if spec else LWD[slot], val)
                        tensor.wait_ge(sem["GMUL"], HC * p + 4 * j + 4)
                        for q in range(4):
                            hc = 4 * j + q
                            for g in range(G):
                                bank = (s % 2) * 4 + g
                                if hc == 0:
                                    if s >= 2:
                                        tensor.wait_ge(sem["EV_DN"], 16 * p + 4 * (s - 2) + g + 1)
                                    elif s == 0:
                                        tensor.wait_ge(sem["EV_AT"],
                                                       16 * p + 13 + (2, 3, 0, 1)[g])
                                    else:
                                        if g < 2:
                                            tensor.wait_ge(sem["SILU"], HC * p + 43 + g)
                                        else:
                                            tensor.wait_ge(sem["GMUL"], HC * p + 41 + g)
                                ins = tensor.matmul(PS[:, bank, :],
                                                    GFM[:, hc, g * 128:(g + 1) * 128],
                                                    WDB[u % 3][:, q * 512:(q + 1) * 512],
                                                    start=(hc == 0), stop=(hc == HC - 1))
                                if hc == HC - 1 and g == G - 1:
                                    ins.then_inc(sem["MM_DN"], 1)
                                elif hc == HC - 1 and g == G - 2:
                                    ins.then_inc(sem["FR_D"], 1)
                        if j < 10:
                            ins.then_inc(sem["FR_D"], 1)

        return nc


def kernel(L0_attn_x, L0_attn_B, L0_attn_C, L0_attn_D,
           L0_ffn_wq_gate, L0_ffn_s_gate, L0_ffn_wq_up, L0_ffn_s_up,
           L0_ffn_wq_down, L0_ffn_s_down):
    x = np.asarray(L0_attn_x, dtype=np.float32).reshape(NCORE * TOK, D)
    B = np.ascontiguousarray(np.asarray(L0_attn_B, dtype=np.float32))
    C = np.ascontiguousarray(np.asarray(L0_attn_C, dtype=np.float32))
    Dv = np.asarray(L0_attn_D, dtype=np.float32)
    wg = np.ascontiguousarray(np.asarray(L0_ffn_wq_gate, dtype=np.int32))
    wu = np.ascontiguousarray(np.asarray(L0_ffn_wq_up, dtype=np.int32))
    wd = np.ascontiguousarray(np.asarray(L0_ffn_wq_down, dtype=np.int32))
    sg = np.asarray(L0_ffn_s_gate, dtype=np.float32)
    su = np.asarray(L0_ffn_s_up, dtype=np.float32)
    sd = np.asarray(L0_ffn_s_down, dtype=np.float32)

    sgc = np.ascontiguousarray(sg.reshape(HC, 128).T)
    suc = np.ascontiguousarray(su.reshape(HC, 128).T)
    sdq = np.ascontiguousarray(
        np.broadcast_to(sd.reshape(SL, 1, 1, 512), (SL, 128, 4, 512)).reshape(SL, 128, D))
    dcol = np.ascontiguousarray(Dv.reshape(DC, 128).T)
    ident = np.eye(128, dtype=np.float32).astype(ml_dtypes.bfloat16)

    wgi = np.ascontiguousarray(
        wg.reshape(HC, 128, DC, 128).transpose(0, 3, 2, 1).reshape(HC, 128, DC * 128))
    wui = np.ascontiguousarray(
        wu.reshape(HC, 128, DC, 128).transpose(0, 3, 2, 1).reshape(HC, 128, DC * 128))
    wdi = np.ascontiguousarray(
        wd.reshape(SL, 512, 11, 4, 128).transpose(0, 2, 4, 3, 1).reshape(SL * 11, 128, D))
    bmi = np.ascontiguousarray(
        B.reshape(NCH, 128, DC, 128).transpose(0, 3, 2, 1).reshape(NCH, 128, DC * 128))
    cmi = np.ascontiguousarray(
        C.reshape(SL, 512, NCH, 128).transpose(0, 3, 2, 1).reshape(SL, 128, NCH * 512))

    nc = build_nc(TOK // T)
    common = dict(wgi=wgi, wui=wui, wdi=wdi, bmi=bmi, cmi=cmi, sgc=sgc, suc=suc,
                  sdq=sdq, dcol=dcol, ident=ident)
    in_maps = [dict(xin=np.ascontiguousarray(x[i * TOK:(i + 1) * TOK]), **common)
               for i in range(NCORE)]
    res = run_bass_kernel_spmd(nc, in_maps, core_ids=list(range(NCORE)))
    outs = [res.results[i]["out"] for i in range(NCORE)]
    return np.concatenate(outs, axis=0).reshape(16, 2048, 2048).astype(np.float32)


# version 21
# speedup vs baseline: 1.1574x; 1.0643x over previous
import numpy as np
import ml_dtypes
import concourse.bass as bass
import concourse.mybir as mybir
from concourse.bass_utils import run_bass_kernel_spmd
from contextlib import ExitStack

F32 = mybir.dt.float32
BF16 = mybir.dt.bfloat16
I32 = mybir.dt.int32
U8 = mybir.dt.uint8
AF = mybir.ActivationFunctionType
ALU = mybir.AluOpType

NCORE = 8
TOK = 4096
D = 2048
NST = 1024
HID = 5632
T = 512
G = 4
DC = D // 128
NCH = NST // 128
HC = HID // 128
SL = 4
NXB = 7
EPS = 1e-5

NU_B, NU_C, NU_G, NU_U, NU_D = NCH, SL, HC, HC, HC
NUNITS = NU_B + NU_C + NU_G + NU_U + NU_D


def unit_kind(u):
    if u < NU_B:
        return ("B", u)
    u -= NU_B
    if u < NU_C:
        return ("C", u)
    u -= NU_C
    if u < NU_G:
        return ("G", u)
    u -= NU_G
    if u < NU_U:
        return ("U", u)
    return ("D", u - NU_U)


def build_nc(n_pass):
    nc = bass.Bass("TRN2", target_bir_lowering=False)

    xin = nc.dram_tensor("xin", [TOK, D], F32, kind="ExternalInput").ap()
    wgi = nc.dram_tensor("wgi", [HC, 128, DC * 128], U8, kind="ExternalInput").ap()
    wui = nc.dram_tensor("wui", [HC, 128, DC * 128], U8, kind="ExternalInput").ap()
    wdi = nc.dram_tensor("wdi", [SL * 11, 128, D], U8, kind="ExternalInput").ap()
    bmi = nc.dram_tensor("bmi", [NCH, 128, DC * 128], F32, kind="ExternalInput").ap()
    cmi = nc.dram_tensor("cmi", [SL, 128, NCH * 512], F32, kind="ExternalInput").ap()
    sgc = nc.dram_tensor("sgc", [128, HC], F32, kind="ExternalInput").ap()
    suc = nc.dram_tensor("suc", [128, HC], F32, kind="ExternalInput").ap()
    sdq = nc.dram_tensor("sdq", [SL, 128, D], F32, kind="ExternalInput").ap()
    dcol = nc.dram_tensor("dcol", [128, DC], F32, kind="ExternalInput").ap()
    ident = nc.dram_tensor("ident", [128, 128], BF16, kind="ExternalInput").ap()
    out = nc.dram_tensor("out", [TOK, D], F32, kind="ExternalOutput").ap()

    sg_img = nc.dram_tensor("sg_img", [HC, 128, DC, 128], BF16, kind="Internal").ap()
    su_img = nc.dram_tensor("su_img", [HC, 128, DC, 128], BF16, kind="Internal").ap()
    sb_img = nc.dram_tensor("sb_img", [NCH, 128, DC, 128], BF16, kind="Internal").ap()
    sc_img = nc.dram_tensor("sc_img", [SL, 128, NCH, 512], BF16, kind="Internal").ap()
    sd_img = nc.dram_tensor("sd_img", [SL * 11, 128, D], BF16, kind="Internal").ap()

    with ExitStack() as ctx:
        E = ctx.enter_context

        SGC = E(nc.sbuf_tensor("SGC", [128, HC], F32))
        SUC = E(nc.sbuf_tensor("SUC", [128, HC], F32))
        DCOL = E(nc.sbuf_tensor("DCOL", [128, DC], F32))
        IDENT = E(nc.sbuf_tensor("IDENT", [128, 128], BF16))
        DDIAG = E(nc.sbuf_tensor("DDIAG", [128, DC, 128], BF16))

        S1T = E(nc.sbuf_tensor("S1T", [128, 8], F32))
        S2T = E(nc.sbuf_tensor("S2T", [128, 8], F32))
        MEANT = E(nc.sbuf_tensor("MEANT", [128, 8], F32))
        MSQT = E(nc.sbuf_tensor("MSQT", [128, 8], F32))
        VART = E(nc.sbuf_tensor("VART", [128, 8], F32))
        STDT = E(nc.sbuf_tensor("STDT", [128, 8], F32))
        RSTDT = E(nc.sbuf_tensor("RSTDT", [128, 8], F32))
        S1PT = E(nc.sbuf_tensor("S1PT", [128, G, SL], F32))
        TR4 = E(nc.sbuf_tensor("TR4", [128, SL], F32))

        with ExitStack() as pctx:
            P_ = pctx.enter_context
            LQ = [P_(nc.sbuf_tensor(f"LQ{i}", [128, D], U8)) for i in range(10)]
            BF32 = [P_(nc.sbuf_tensor(f"BF32{i}", [128, 2 * D], F32)) for i in range(2)]
            WOUT = [P_(nc.sbuf_tensor(f"WOUT{i}", [128, 2 * D], BF16)) for i in range(8)]
            SDP = P_(nc.sbuf_tensor("SDP", [128, D], F32))

        XT = [E(nc.sbuf_tensor(f"XT{b}", [128, D], F32)) for b in range(NXB)]
        HTM = [E(nc.sbuf_tensor(f"HTM{i}", [128, D], BF16)) for i in range(2)]
        HFM1 = E(nc.sbuf_tensor("HFM1", [128, DC, T], BF16))
        HFM2 = E(nc.sbuf_tensor("HFM2", [128, DC, T], BF16))
        SFM = E(nc.sbuf_tensor("SFM", [128, NCH, T], BF16))
        GFM = E(nc.sbuf_tensor("GFM", [128, HC, T], BF16))
        WGU = [E(nc.sbuf_tensor(f"WGU{i}", [128, DC, 128], BF16)) for i in range(3)]
        WBB = [E(nc.sbuf_tensor(f"WBB{i}", [128, DC, 128], BF16)) for i in range(3)]
        WCB = [E(nc.sbuf_tensor(f"WCB{i}", [128, NCH // 2, T], BF16)) for i in range(4)]
        WDB = [E(nc.sbuf_tensor(f"WDB{i}", [128, D], BF16)) for i in range(3)]

        PS = E(nc.psum_tensor("PS", [128, 8, 512], F32))

        sem = {}
        for name in ["LDC", "LSDP", "P_DQ",
                     "MM_ST", "MM_AT", "MM_G", "MM_U", "MM_DN", "FR_D",
                     "EV_AT", "EV_DN", "TANH", "SILU", "GMUL",
                     "APPLY", "SQ_A", "FIN", "SQRT_A", "DDIAG_OK", "DS"]:
            sem[name] = E(nc.semaphore(name))
        PLD1 = [E(nc.semaphore(f"PLD1_{i}")) for i in range(4)]
        PLD2 = [E(nc.semaphore(f"PLD2_{i}")) for i in range(10)]
        PWR = [E(nc.semaphore(f"PWR{i}")) for i in range(8)]
        LDX = [E(nc.semaphore(f"LDX{i}")) for i in range(NXB)]
        STX = [E(nc.semaphore(f"STX{i}")) for i in range(NXB)]
        LWB = [E(nc.semaphore(f"LWB{i}")) for i in range(3)]
        LWC = [E(nc.semaphore(f"LWC{i}")) for i in range(4)]
        LWGU = [E(nc.semaphore(f"LWGU{i}")) for i in range(3)]
        LWD = [E(nc.semaphore(f"LWD{i}")) for i in range(3)]
        LWDS = [E(nc.semaphore(f"LWDS{i}")) for i in range(3)]
        TPS = [E(nc.semaphore(f"TPS{i}")) for i in range(4)]

        block = E(nc.Block())
        PWR_FINAL = [16 * len([1 for u in range(NUNITS) if u % 8 == i])
                     for i in range(8)]

        wd_info = {}
        _cnt_wd = {}
        for _p in range(n_pass):
            for _s in range(SL):
                for _j in range(11):
                    _u = HC * _p + 11 * _s + _j
                    special = _u >= 3 and (_u - 3) % 11 == 10
                    key = ("S" if special else "H", _u % 3)
                    _cnt_wd[key] = _cnt_wd.get(key, 0) + 16
                    wd_info[_u] = (special, _u % 3, _cnt_wd[key])

        AT_TILES = [(g, 2 * sp + si) for sp in range(2) for g in range(G)
                    for si in range(2)]
        AT_BANK = [2 * (s % 2) + (g % 2) for (g, s) in AT_TILES]

        ST_ORDER = []
        _a = _b = 0
        while _a < NCH or _b < NCH:
            if _a < NCH and (_a - _b < 2 or _b >= NCH):
                ST_ORDER.append((_a, 0))
                _a += 1
            else:
                ST_ORDER.append((_b, 1))
                _b += 1
        ST_POS = {nh: c for c, nh in enumerate(ST_ORDER)}

        def xb(p, g):
            return (4 * p + g) % NXB

        def ldx_val(p, g):
            return 16 * ((4 * p + g) // NXB + 1)

        @block.sync
        def _(sync):
            for apsrc, sb in [(sgc, SGC), (suc, SUC),
                              (dcol, DCOL), (ident, IDENT)]:
                sync.dma_start(out=sb[:], in_=apsrc).then_inc(sem["LDC"], 16)

            def emit_prep_load(u):
                kind, i = unit_kind(u)
                if kind == "D" and i % 11 == 0:
                    s_ = i // 11
                    if s_ > 0:
                        sync.wait_ge(sem["P_DQ"], NU_B + NU_C + NU_G + NU_U + 11 * s_)
                    sync.dma_start(out=SDP[:], in_=sdq[s_]).then_inc(sem["LSDP"], 16)
                if kind in ("B", "C"):
                    if u >= 2:
                        sync.wait_ge(sem["P_DQ"], u - 1)
                    w = D if kind == "B" else NCH * 512
                    src = bmi[i] if kind == "B" else cmi[i]
                    sync.dma_start(out=BF32[u % 2][:, :w], in_=src).then_inc(PLD1[u % 2], 16)
                else:
                    o = u - (NU_B + NU_C)
                    if o >= 10:
                        sync.wait_ge(sem["P_DQ"], u - 10 + 1)
                    src = wgi[i] if kind == "G" else (wui[i] if kind == "U" else wdi[i])
                    sync.dma_start(out=LQ[o % 10][:], in_=src).then_inc(PLD2[o % 10], 16)

            for u in range(NUNITS):
                emit_prep_load(u)

            for i in range(8):
                sync.wait_ge(PWR[i], PWR_FINAL[i])

            def emit_b(p, half, n):
                u = 16 * p + 8 * half + n
                if u >= 3:
                    sync.wait_ge(sem["MM_ST"], u - 2)
                sync.dma_start(out=WBB[u % 3][:], in_=sb_img[n]).then_inc(LWB[u % 3], 16)

            def emit_c(p, sp, si, h2_):
                s = 2 * sp + si
                slot = 2 * si + h2_
                if p > 0 or sp > 0:
                    pm = (16 * p + 8 * (sp - 1) + 7 + si) if sp > 0 \
                        else (16 * (p - 1) + 15 + si)
                    sync.wait_ge(sem["MM_AT"], pm)
                sync.dma_start(out=WCB[slot][:],
                               in_=sc_img[s][:, h2_ * 4:(h2_ + 1) * 4, :]
                               ).then_inc(LWC[slot], 16)

            def emit_gu(p, hc, is_up):
                t = 2 * (HC * p + hc) + is_up
                if t >= 3:
                    hc3 = (t - 3) // 2
                    if (t - 3) % 2 == 0:
                        sync.wait_ge(sem["MM_G"], hc3 + 1)
                    else:
                        sync.wait_ge(sem["MM_U"], hc3 + 1)
                sync.dma_start(out=WGU[t % 3][:],
                               in_=(su_img if is_up else sg_img)[hc]
                               ).then_inc(LWGU[t % 3], 16)

            for p in range(n_pass):
                for half in range(2):
                    for n in range(NCH):
                        if p > 0 and half == 0 and n < 3:
                            continue
                        emit_b(p, half, n)
                for sp in range(2):
                    for si in range(2):
                        for h2_ in range(2):
                            if p > 0 and sp == 0:
                                continue
                            emit_c(p, sp, si, h2_)
                for hc in range(HC):
                    for is_up in range(2):
                        if p > 0 and 2 * hc + is_up < 3:
                            continue
                        emit_gu(p, hc, is_up)
                if p + 1 < n_pass:
                    for n in range(3):
                        emit_b(p + 1, 0, n)
                    for si in range(2):
                        for h2_ in range(2):
                            emit_c(p + 1, 0, si, h2_)
                for s in range(SL):
                    for j in range(11):
                        u = HC * p + 11 * s + j
                        if wd_info[u][0]:
                            continue
                        if u >= 3:
                            sync.wait_ge(sem["FR_D"], u - 2)
                        sync.dma_start(out=WDB[u % 3][:],
                                       in_=sd_img[s * 11 + j]).then_inc(LWD[u % 3], 16)
                if p + 1 < n_pass:
                    for q in range(3):
                        emit_gu(p + 1, q // 2, q % 2)
            for b in range(NXB):
                n_st = len([1 for gi in range(4 * n_pass) if gi % NXB == b])
                if n_st:
                    sync.wait_ge(STX[b], 16 * n_st)

        @block.gpsimd
        def _(gpsimd):
            for u in range(NUNITS):
                kind, i = unit_kind(u)
                gpsimd.wait_ge(sem["P_DQ"], u + 1)
                if kind == "B":
                    dst, w = sb_img[i], D
                elif kind == "C":
                    dst, w = sc_img[i], NCH * 512
                elif kind == "G":
                    dst, w = sg_img[i], D
                elif kind == "U":
                    dst, w = su_img[i], D
                else:
                    dst, w = sd_img[i], D
                gpsimd.dma_start(out=dst, in_=WOUT[u % 8][:, :w]).then_inc(PWR[u % 8], 16)

            for p in range(n_pass):
                for s in range(SL):
                    for j in range(11):
                        u = HC * p + 11 * s + j
                        if not wd_info[u][0]:
                            continue
                        gpsimd.wait_ge(sem["FR_D"], u - 2)
                        pu = u - 3
                        pp, ps = pu // 44, (pu % 44) // 11
                        gpsimd.wait_ge(sem["MM_DN"], 4 * pp + ps + 1)
                        gpsimd.dma_start(out=WDB[u % 3][:],
                                         in_=sd_img[s * 11 + j]).then_inc(LWDS[u % 3], 16)

        @block.scalar
        def _(scalar):
            for i in range(8):
                scalar.wait_ge(PWR[i], PWR_FINAL[i])
            scalar.wait_ge(sem["LDC"], 64)

            def emit_xload(pp, g):
                gi = 4 * pp + g
                b = gi % NXB
                if gi >= NXB:
                    scalar.wait_ge(STX[b], 16 * (gi // NXB))
                scalar.dma_start(out=XT[b][:],
                                 in_=xin[gi * 128:(gi + 1) * 128, :]).then_inc(LDX[b], 16)

            def act_ln1(pp, groups):
                for g in groups:
                    k = 8 * pp + g
                    scalar.wait_ge(sem["FIN"], k + 1)
                    scalar.activation(STDT[:, g:g + 1], VART[:, g:g + 1],
                                      AF.Sqrt).then_inc(sem["SQRT_A"], 1)
                    scalar.wait_ge(sem["APPLY"], k + 1)
                    if pp > 0:
                        scalar.wait_ge(sem["MM_AT"], 16 * pp)
                    scalar.dma_start(out=HFM1[:, :, g * 128:(g + 1) * 128],
                                     in_=HTM[k % 2][:], transpose=True).then_inc(TPS[k % 4], 16)

            for g in range(G):
                emit_xload(0, g)
            act_ln1(0, (0, 1, 2, 3))
            for p in range(n_pass):
                for n in range(NCH):
                    if n == 0 and p > 0:
                        scalar.wait_ge(sem["MM_AT"], 16 * p)
                    scalar.wait_ge(sem["MM_ST"], 16 * p + 8 + n + 1)
                    scalar.activation(SFM[:, n, :], PS[:, n, :],
                                      AF.Tanh).then_inc(sem["TANH"], 1)
                for g in range(G):
                    k = 8 * p + 4 + g
                    scalar.wait_ge(sem["EV_AT"], 16 * p + 2 * g + 10)
                    scalar.wait_ge(TPS[(k - 2) % 4], 16 * ((k - 2) // 4 + 1))
                    scalar.activation(HTM[k % 2][:], XT[xb(p, g)][:], AF.Square,
                                      accum_out=S2T[:, 4 + g:5 + g]).then_inc(sem["SQ_A"], 1)
                    scalar.wait_ge(sem["FIN"], k + 1)
                    scalar.activation(STDT[:, 4 + g:5 + g], VART[:, 4 + g:5 + g],
                                      AF.Sqrt).then_inc(sem["SQRT_A"], 1)
                    scalar.wait_ge(sem["APPLY"], k + 1)
                    if p > 0:
                        scalar.wait_ge(sem["MM_U"], HC * p)
                    scalar.dma_start(out=HFM2[:, :, g * 128:(g + 1) * 128],
                                     in_=HTM[k % 2][:], transpose=True).then_inc(TPS[k % 4], 16)
                if p + 1 < n_pass:
                    for g in range(3):
                        emit_xload(p + 1, g)
                for hc in range(HC):
                    if hc == 0 and p > 0:
                        scalar.wait_ge(sem["MM_DN"], 4 * p)
                    scalar.wait_ge(sem["MM_G"], HC * p + hc + 1)
                    scalar.activation(GFM[:, hc, :], PS[:, 4 + hc % 2, :],
                                      AF.Silu, scale=SGC[:, hc:hc + 1]).then_inc(sem["SILU"], 1)
                if p + 1 < n_pass:
                    act_ln1(p + 1, (0, 1, 2))
                scalar.wait_ge(sem["EV_DN"], 16 * p + 13)
                scalar.dma_start(out=out[(4 * p) * 128:(4 * p + 1) * 128, :],
                                 in_=XT[xb(p, 0)][:]).then_inc(STX[xb(p, 0)], 16)
                if p + 1 < n_pass:
                    emit_xload(p + 1, 3)
                    act_ln1(p + 1, (3,))
                for g in (1, 2, 3):
                    scalar.wait_ge(sem["EV_DN"], 16 * p + 13 + g)
                    scalar.dma_start(out=out[(4 * p + g) * 128:(4 * p + g + 1) * 128, :],
                                     in_=XT[xb(p, g)][:]).then_inc(STX[xb(p, g)], 16)

        @block.vector
        def _(vector):
            for u in range(NUNITS):
                kind, i = unit_kind(u)
                if u >= 8:
                    vector.wait_ge(PWR[u % 8], 16 * (u // 8))
                if kind in ("B", "C"):
                    w = D if kind == "B" else NCH * 512
                    vector.wait_ge(PLD1[u % 2], 16 * (u // 2 + 1))
                    vector.tensor_copy(WOUT[u % 8][:, :w],
                                       BF32[u % 2][:, :w]).then_inc(sem["P_DQ"], 1)
                elif kind in ("G", "U"):
                    o = u - (NU_B + NU_C)
                    vector.wait_ge(PLD2[o % 10], 16 * (o // 10 + 1))
                    vector.tensor_scalar_add(WOUT[u % 8][:, :D],
                                             LQ[o % 10][:], -128.0).then_inc(sem["P_DQ"], 1)
                else:
                    o = u - (NU_B + NU_C)
                    vector.wait_ge(PLD2[o % 10], 16 * (o // 10 + 1))
                    vector.wait_ge(sem["LSDP"], 16 * (i // 11 + 1))
                    vector.scalar_tensor_tensor(WOUT[u % 8][:, :D], LQ[o % 10][:],
                                                -128.0, SDP[:], ALU.add,
                                                ALU.mult).then_inc(sem["P_DQ"], 1)

            for i in range(8):
                vector.wait_ge(PWR[i], PWR_FINAL[i])
            vector.wait_ge(sem["LDC"], 64)
            ins = None
            for c in range(DC):
                ins = vector.tensor_scalar_mul(DDIAG[:, c, :], IDENT[:], DCOL[:, c:c + 1])
            ins.then_inc(sem["DDIAG_OK"], 1)

            dsc = {"n": 0}

            def ds_inc(ins):
                dsc["n"] += 1
                ins.then_inc(sem["DS"], 1)

            def ds_wait():
                vector.wait_ge(sem["DS"], dsc["n"])

            def ln_finalize(k, i, sqa=None):
                if sqa is not None:
                    vector.wait_ge(sem["SQ_A"], sqa)
                ds_wait()
                ds_inc(vector.tensor_scalar_mul(MEANT[:, i:i + 1], S1T[:, i:i + 1], 1.0 / D))
                ds_wait()
                ds_inc(vector.tensor_scalar(MSQT[:, i:i + 1], MEANT[:, i:i + 1],
                                            MEANT[:, i:i + 1], -EPS, ALU.mult, ALU.add))
                ds_wait()
                vector.scalar_tensor_tensor(VART[:, i:i + 1], S2T[:, i:i + 1], 1.0 / D,
                                            MSQT[:, i:i + 1], ALU.mult,
                                            ALU.subtract).then_inc(sem["FIN"], 1)
                vector.wait_ge(sem["SQRT_A"], k + 1)
                ds_inc(vector.reciprocal(RSTDT[:, i:i + 1], STDT[:, i:i + 1]))

            def htm_free(k):
                if k >= 2:
                    vector.wait_ge(TPS[(k - 2) % 4], 16 * ((k - 2) // 4 + 1))

            def ln_apply(k, i, b):
                ds_wait()
                vector.tensor_scalar(HTM[k % 2][:], XT[b][:], MEANT[:, i:i + 1],
                                     RSTDT[:, i:i + 1], ALU.subtract,
                                     ALU.mult).then_inc(sem["APPLY"], 1)

            def dve_ln1(pp, groups):
                for g in groups:
                    k = 8 * pp + g
                    vector.wait_ge(LDX[xb(pp, g)], ldx_val(pp, g))
                    htm_free(k)
                    ds_inc(vector.tensor_scalar(HTM[k % 2][:], XT[xb(pp, g)][:], 1.0, 0.0,
                                                ALU.mult, ALU.add,
                                                accum_out=S1T[:, g:g + 1]))
                    ds_wait()
                    ds_inc(vector.scalar_tensor_tensor(HTM[k % 2][:], XT[xb(pp, g)][:],
                                                       1.0, XT[xb(pp, g)][:], ALU.mult,
                                                       ALU.mult,
                                                       accum_out=S2T[:, g:g + 1]))
                    ln_finalize(k, g)
                    ln_apply(k, g, xb(pp, g))

            dve_ln1(0, (0, 1, 2, 3))
            for p in range(n_pass):
                def at_evac(m):
                    g, s = AT_TILES[m]
                    vector.wait_ge(sem["MM_AT"], 16 * p + m + 1)
                    vector.scalar_tensor_tensor(
                        XT[xb(p, g)][:, s * 512:(s + 1) * 512], PS[:, AT_BANK[m], :],
                        1.0, XT[xb(p, g)][:, s * 512:(s + 1) * 512], ALU.mult, ALU.add,
                        accum_out=S1PT[:, g, s:s + 1]).then_inc(sem["EV_AT"], 1)

                for m in range(8):
                    at_evac(m)
                for g in range(G):
                    at_evac(8 + 2 * g)
                    at_evac(8 + 2 * g + 1)
                    k = 8 * p + 4 + g
                    vector.wait_ge(sem["EV_AT"], 16 * p + 2 * g + 10)
                    htm_free(k)
                    ds_inc(vector.tensor_scalar(TR4[:], S1PT[:, g, :], 1.0, 0.0, ALU.mult,
                                                ALU.add, accum_out=S1T[:, 4 + g:5 + g]))
                    ln_finalize(k, 4 + g, sqa=4 * p + g + 1)
                    ln_apply(k, 4 + g, xb(p, g))
                for hc in range(HC):
                    vector.wait_ge(sem["SILU"], HC * p + hc + 1)
                    vector.wait_ge(sem["MM_U"], HC * p + hc + 1)
                    vector.scalar_tensor_tensor(GFM[:, hc, :], PS[:, 6 + hc % 2, :],
                                                SUC[:, hc:hc + 1], GFM[:, hc, :],
                                                ALU.mult, ALU.mult).then_inc(sem["GMUL"], 1)
                if p + 1 < n_pass:
                    dve_ln1(p + 1, (0, 1, 2))
                for s in range(SL):
                    for g in range(G):
                        vector.wait_ge(sem["MM_DN"], 4 * p + s + 1)
                        vector.tensor_tensor(XT[xb(p, g)][:, s * 512:(s + 1) * 512],
                                             XT[xb(p, g)][:, s * 512:(s + 1) * 512],
                                             PS[:, (s % 2) * 4 + g, :],
                                             ALU.add).then_inc(sem["EV_DN"], 1)
                if p + 1 < n_pass:
                    dve_ln1(p + 1, (3,))

        @block.tensor
        def _(tensor):
            tensor.wait_ge(sem["DDIAG_OK"], 1)
            for p in range(n_pass):
                for half in range(2):
                    lo, hi = half * 256, half * 256 + 256
                    for n in range(NCH):
                        if n == 0:
                            for i in ((0, 1) if half == 0 else (2, 3)):
                                tensor.wait_ge(TPS[i], 16 * (2 * p + 1))
                        u = 16 * p + 8 * half + n
                        tensor.wait_ge(LWB[u % 3], 16 * (u // 3 + 1))
                        if half == 0 and p > 0:
                            tensor.wait_ge(sem["EV_DN"], 16 * (p - 1) + 9 + n)
                        ins = None
                        for c in range(DC):
                            ins = tensor.matmul(PS[:, n, lo:hi], WBB[u % 3][:, c, :],
                                                HFM1[:, c, lo:hi],
                                                start=(c == 0), stop=(c == DC - 1))
                        ins.then_inc(sem["MM_ST"], 1)
                for m in range(16):
                    g, s = AT_TILES[m]
                    bank = AT_BANK[m]
                    if m == 0:
                        tensor.wait_ge(sem["TANH"], 8 * p + 8)
                    if 16 * p + m - 3 >= 1:
                        tensor.wait_ge(sem["EV_AT"], 16 * p + m - 3)
                    for sub in range(4):
                        dcg = s * 4 + sub
                        tensor.matmul(PS[:, bank, sub * 128:(sub + 1) * 128],
                                      HFM1[:, dcg, g * 128:(g + 1) * 128],
                                      DDIAG[:, dcg, :],
                                      start=(sub == 0), stop=False)
                    ins = None
                    for n in range(NCH):
                        if n % 4 == 0:
                            tensor.wait_ge(LWC[2 * (s % 2) + n // 4],
                                           16 * (2 * p + s // 2 + 1))
                        ins = tensor.matmul(PS[:, bank, :],
                                            SFM[:, n, g * 128:(g + 1) * 128],
                                            WCB[2 * (s % 2) + n // 4][:, n % 4, :],
                                            start=False, stop=(n == NCH - 1))
                    ins.then_inc(sem["MM_AT"], 1)

                def gate_chain(hc, lo, hi, inc):
                    t = 2 * (HC * p + hc)
                    ins = None
                    for c in range(DC):
                        ins = tensor.matmul(PS[:, 4 + hc % 2, lo:hi], WGU[t % 3][:, c, :],
                                            HFM2[:, c, lo:hi],
                                            start=(c == 0), stop=(c == DC - 1))
                    if inc:
                        ins.then_inc(sem["MM_G"], 1)

                def up_chain(hc, lo, hi, inc):
                    t = 2 * (HC * p + hc) + 1
                    ins = None
                    for c in range(DC):
                        ins = tensor.matmul(PS[:, 6 + hc % 2, lo:hi], WGU[t % 3][:, c, :],
                                            HFM2[:, c, lo:hi],
                                            start=(c == 0), stop=(c == DC - 1))
                    if inc:
                        ins.then_inc(sem["MM_U"], 1)

                for hc in (0, 1):
                    if hc == 0:
                        tensor.wait_ge(TPS[0], 16 * (2 * p + 2))
                        tensor.wait_ge(TPS[1], 16 * (2 * p + 2))
                    t = 2 * (HC * p + hc)
                    tensor.wait_ge(LWGU[t % 3], 16 * (t // 3 + 1))
                    if p > 0:
                        tensor.wait_ge(sem["EV_DN"], 16 * (p - 1) + 13 + hc)
                    gate_chain(hc, 0, 256, False)
                for hc in (0, 1):
                    if hc == 0:
                        tensor.wait_ge(TPS[2], 16 * (2 * p + 2))
                        tensor.wait_ge(TPS[3], 16 * (2 * p + 2))
                    gate_chain(hc, 256, 512, True)
                for hc in (0, 1):
                    t = 2 * (HC * p + hc) + 1
                    tensor.wait_ge(LWGU[t % 3], 16 * (t // 3 + 1))
                    if p > 0:
                        tensor.wait_ge(sem["EV_DN"], 16 * (p - 1) + 15 + hc)
                    up_chain(hc, 0, 512, True)
                for hc in range(2, HC):
                    t = 2 * (HC * p + hc)
                    tensor.wait_ge(LWGU[t % 3], 16 * (t // 3 + 1))
                    tensor.wait_ge(sem["SILU"], HC * p + hc - 1)
                    gate_chain(hc, 0, 512, True)
                    tensor.wait_ge(LWGU[(t + 1) % 3], 16 * ((t + 1) // 3 + 1))
                    tensor.wait_ge(sem["GMUL"], HC * p + hc - 1)
                    up_chain(hc, 0, 512, True)

                for s in range(SL):
                    for j in range(11):
                        u = HC * p + 11 * s + j
                        spec, slot, val = wd_info[u]
                        tensor.wait_ge(LWDS[slot] if spec else LWD[slot], val)
                        tensor.wait_ge(sem["GMUL"], HC * p + 4 * j + 4)
                        for q in range(4):
                            hc = 4 * j + q
                            for g in range(G):
                                bank = (s % 2) * 4 + g
                                if hc == 0:
                                    if s >= 2:
                                        tensor.wait_ge(sem["EV_DN"], 16 * p + 4 * (s - 2) + g + 1)
                                    elif s == 0:
                                        tensor.wait_ge(sem["EV_AT"],
                                                       16 * p + 13 + (2, 3, 0, 1)[g])
                                    else:
                                        if g < 2:
                                            tensor.wait_ge(sem["SILU"], HC * p + 43 + g)
                                        else:
                                            tensor.wait_ge(sem["GMUL"], HC * p + 41 + g)
                                ins = tensor.matmul(PS[:, bank, :],
                                                    GFM[:, hc, g * 128:(g + 1) * 128],
                                                    WDB[u % 3][:, q * 512:(q + 1) * 512],
                                                    start=(hc == 0), stop=(hc == HC - 1))
                                if hc == HC - 1 and g == G - 1:
                                    ins.then_inc(sem["MM_DN"], 1)
                                elif hc == HC - 1 and g == G - 2:
                                    ins.then_inc(sem["FR_D"], 1)
                        if j < 10:
                            ins.then_inc(sem["FR_D"], 1)

        return nc


def kernel(L0_attn_x, L0_attn_B, L0_attn_C, L0_attn_D,
           L0_ffn_wq_gate, L0_ffn_s_gate, L0_ffn_wq_up, L0_ffn_s_up,
           L0_ffn_wq_down, L0_ffn_s_down):
    x = np.asarray(L0_attn_x, dtype=np.float32).reshape(NCORE * TOK, D)
    B = np.ascontiguousarray(np.asarray(L0_attn_B, dtype=np.float32))
    C = np.ascontiguousarray(np.asarray(L0_attn_C, dtype=np.float32))
    Dv = np.asarray(L0_attn_D, dtype=np.float32)
    wg = np.ascontiguousarray(np.asarray(L0_ffn_wq_gate, dtype=np.int32).astype(np.uint8))
    wu = np.ascontiguousarray(np.asarray(L0_ffn_wq_up, dtype=np.int32).astype(np.uint8))
    wd = np.ascontiguousarray(np.asarray(L0_ffn_wq_down, dtype=np.int32).astype(np.uint8))
    sg = np.asarray(L0_ffn_s_gate, dtype=np.float32)
    su = np.asarray(L0_ffn_s_up, dtype=np.float32)
    sd = np.asarray(L0_ffn_s_down, dtype=np.float32)

    sgc = np.ascontiguousarray(sg.reshape(HC, 128).T)
    suc = np.ascontiguousarray(su.reshape(HC, 128).T)
    sdq = np.ascontiguousarray(
        np.broadcast_to(sd.reshape(SL, 1, 1, 512), (SL, 128, 4, 512)).reshape(SL, 128, D))
    dcol = np.ascontiguousarray(Dv.reshape(DC, 128).T)
    ident = np.eye(128, dtype=np.float32).astype(ml_dtypes.bfloat16)

    wgi = np.ascontiguousarray(
        wg.reshape(HC, 128, DC, 128).transpose(0, 3, 2, 1).reshape(HC, 128, DC * 128))
    wui = np.ascontiguousarray(
        wu.reshape(HC, 128, DC, 128).transpose(0, 3, 2, 1).reshape(HC, 128, DC * 128))
    wdi = np.ascontiguousarray(
        wd.reshape(SL, 512, 11, 4, 128).transpose(0, 2, 4, 3, 1).reshape(SL * 11, 128, D))
    bmi = np.ascontiguousarray(
        B.reshape(NCH, 128, DC, 128).transpose(0, 3, 2, 1).reshape(NCH, 128, DC * 128))
    cmi = np.ascontiguousarray(
        C.reshape(SL, 512, NCH, 128).transpose(0, 3, 2, 1).reshape(SL, 128, NCH * 512))

    nc = build_nc(TOK // T)
    common = dict(wgi=wgi, wui=wui, wdi=wdi, bmi=bmi, cmi=cmi, sgc=sgc, suc=suc,
                  sdq=sdq, dcol=dcol, ident=ident)
    in_maps = [dict(xin=np.ascontiguousarray(x[i * TOK:(i + 1) * TOK]), **common)
               for i in range(NCORE)]
    res = run_bass_kernel_spmd(nc, in_maps, core_ids=list(range(NCORE)))
    outs = [res.results[i]["out"] for i in range(NCORE)]
    return np.concatenate(outs, axis=0).reshape(16, 2048, 2048).astype(np.float32)
